# version 03
# baseline (speedup 1.0000x reference)
import numpy as np
import ml_dtypes
import contextlib
import concourse.bass as bass
import concourse.mybir as mybir
from concourse.bass_utils import run_bass_kernel_spmd

F32 = mybir.dt.float32
BF16 = mybir.dt.bfloat16
AF = mybir.ActivationFunctionType
ALU = mybir.AluOpType
AX = mybir.AxisListType
BFNP = ml_dtypes.bfloat16

T = 4096
D = 1024
NFC = 8
TT = 1024
NTT = T // TT
EPS = 1e-5
NCORES = 8

ENGINES = ("sync", "tensor", "vector", "scalar", "gpsimd")
_PROG_CACHE = {}


class Em:
    def __init__(self, nc, serial=False):
        self.nc = nc
        self.serial = serial
        self.streams = {e: [] for e in ENGINES}
        self.cnt = {}
        self.handles = {}
        self.last_wait = {e: {} for e in ENGINES}
        self.phase = 0

    def sems(self, stack, *names):
        self.phase += 1
        for n in names:
            self.handles[n] = stack.enter_context(
                self.nc.semaphore(f"{n}_p{self.phase}"))
            self.cnt[n] = 0
        self.last_wait = {e: {} for e in ENGINES}

    def new_phase(self, stack):
        self.barrier()
        self.sems(stack, "ld", "st", "pe", "dv", "ac", "gp")

    def op(self, eng, fn, waits=(), inc=None, amt=1):
        if self.serial:
            mine = {"sync": ("ld",), "tensor": ("pe",), "vector": ("dv",),
                    "scalar": ("ac",), "gpsimd": ("gp", "st")}[eng]
            waits = tuple((s, self.cnt[s]) for s in self.cnt if s not in mine)
        w = []
        for s, v in waits:
            if v and v > 0 and self.last_wait[eng].get(s, 0) < v:
                w.append((self.handles[s], v))
                self.last_wait[eng][s] = v
        if inc is not None:
            self.cnt[inc] += amt
        self.streams[eng].append(
            (w, fn, self.handles[inc] if inc is not None else None, amt))
        return self.cnt[inc] if inc is not None else None

    def dma(self, eng, out, in_, waits=(), inc="ld"):
        return self.op(eng, lambda e, o=out, i=in_: e.dma_start(out=o, in_=i),
                       waits, inc, 16)

    def barrier(self):
        snap = dict(self.cnt)
        mine = {"sync": ("ld",), "tensor": ("pe",), "vector": ("dv",),
                "scalar": ("ac",), "gpsimd": ("gp", "st")}
        for eng in ENGINES:
            waits = tuple((s, v) for s, v in snap.items() if s not in mine[eng])
            self.op(eng, None, waits)

    def emit(self, block):
        for eng in ENGINES:
            stream = self.streams[eng]
            if not stream:
                continue
            dec = getattr(block, eng)

            def make(eng_stream):
                def emit_fn(e):
                    for w, fn, inc_h, amt in eng_stream:
                        for h, v in w:
                            e.wait_ge(h, v)
                        inst = fn(e) if fn is not None else None
                        if inc_h is not None:
                            if inst is None:
                                inst = e.nop()
                            inst.then_inc(inc_h, amt)
                return emit_fn
            dec(make(stream))


def mm_op(em, out, lhsT, rhs, start, stop, waits=()):
    return em.op("tensor",
                 lambda e, o=out, l=lhsT, r=rhs, s1=start, s2=stop:
                 e.matmul(out=o, lhsT=l, rhs=r, start=s1, stop=s2),
                 waits, "pe")


def tr_op(em, out, in_, ident, waits=()):
    return em.op("tensor",
                 lambda e, o=out, i=in_, d=ident:
                 e.transpose(out=o, in_=i, identity=d),
                 waits, "pe")


def tt_op(em, out, in0, in1, aluop, waits=()):
    return em.op("vector",
                 lambda e, o=out, a=in0, b=in1, p=aluop:
                 e.tensor_tensor(out=o, in0=a, in1=b, op=p),
                 waits, "dv")


def ts_op(em, out, in0, s1, aluop, waits=()):
    return em.op("vector",
                 lambda e, o=out, a=in0, x=s1, p=aluop:
                 e.tensor_scalar(out=o, in0=a, scalar1=x, scalar2=None, op0=p),
                 waits, "dv")


def red_op(em, out, in_, waits=(), op="max"):
    if op == "max":
        return em.op("vector",
                     lambda e, o=out, i=in_: e.reduce_max(out=o, in_=i, axis=AX.X),
                     waits, "dv")
    return em.op("vector",
                 lambda e, o=out, i=in_: e.reduce_sum(out=o, in_=i, axis=AX.X),
                 waits, "dv")


def cp_op(em, out, in_, waits=(), eng="vector", sem="dv"):
    return em.op(eng, lambda e, o=out, i=in_: e.tensor_copy(out=o, in_=i),
                 waits, sem)


def rc_op(em, out, in_, waits=()):
    return em.op("vector", lambda e, o=out, i=in_: e.reciprocal(out=o, in_=i),
                 waits, "dv")


def ms_op(em, ap, val, waits=(), eng="vector", sem="dv"):
    return em.op(eng, lambda e, a=ap, v=val: e.memset(a, v), waits, sem)


def act_op(em, out, in_, func, waits=(), bias=None, scale=None, accum=None):
    kw = {}
    if bias is not None:
        kw["bias"] = bias
    if scale is not None:
        kw["scale"] = scale
    if accum is not None:
        kw["accum_out"] = accum
    return em.op("scalar",
                 lambda e, o=out, i=in_, f=func, k=dict(kw):
                 e.activation(out=o, in_=i, func=f, **k),
                 waits, "ac")


class Bufs:
    pass


def alloc_bufs(nc, stack):
    E_ = stack.enter_context
    B = Bufs()
    sb = nc.sbuf_tensor
    B.xtt = E_(sb("xtt", [128, 8 * TT], F32))
    B.h = E_(sb("h", [128, 8 * TT], BF16))
    B.he = E_(sb("he", [128, 8 * TT], BF16))
    B.big = E_(sb("big", [128, 24 * TT], BF16))
    B.sq = E_(sb("sq", [128, TT], F32))
    B.sg = E_(sb("sg", [128, 2 * TT], F32))
    B.wstage = E_(sb("wstage", [128, 2048], F32))
    B.wgu = E_(sb("wgu", [128, 4096], BF16))
    B.wdstage = E_(sb("wdstage", [128, 2816], F32))
    B.wd = E_(sb("wd", [128, 5632], BF16))
    B.cwb = E_(sb("cwb", [128, TT], BF16))
    B.cw9b = E_(sb("cw9b", [9, T], BF16))
    B.wrf = E_(sb("wrf", [128, 80], F32))
    B.sbc = E_(sb("sbc", [128, 16], F32))
    B.srow = E_(sb("srow", [1, 16], F32))
    B.rowsA = E_(sb("rowsA", [1, 2 * TT], F32))
    B.lg = E_(sb("lg", [128, 128], F32))
    B.m2s = E_(sb("m2s", [128, 32], F32))
    B.cwtc = E_(sb("cwtc", [128, 32], F32))
    B.zr = E_(sb("zr", [1, 512], F32))
    B.zrb = E_(sb("zrb", [1, 512], BF16))
    B.zb = E_(sb("zb", [64, 512], F32))
    B.xsl = E_(sb("xsl", [64, 512], F32))
    B.tmp64 = E_(sb("tmp64", [64, 512], F32))
    B.cmx = E_(sb("cmx", [128, 64], F32))
    B.ctb = E_(sb("ctb", [32, 128], BF16))
    B.smT = E_(sb("smT", [64, 128], BF16))
    B.pbf = E_(sb("pbf", [64, 64], BF16))
    B.ones_f = E_(sb("s_ones_f", [128, 128], F32))
    B.ones_b = E_(sb("s_ones_b", [128, 128], BF16))
    B.ident = E_(sb("s_ident", [128, 128], F32))
    B.sel9 = E_(sb("s_sel9", [9, 9 * 128], BF16))
    B.sel8 = E_(sb("s_sel8", [8, 8 * 128], BF16))
    B.triu = E_(sb("s_triu", [64, 64], F32))
    B.qb = B.big[0:65, 0:T]
    B.kb = B.big[0:65, T:2 * T]
    B.vt = B.big[:, 2 * T:2 * T + 32 * 65]
    B.probs = B.big[:, 2 * T + 2080:2 * T + 2080 + 3 * TT]
    B.recip = B.xtt[0:64, 0:T]
    B.psA = E_(nc.psum_tensor("psA", [128, 2048], F32))
    B.psB = E_(nc.psum_tensor("psB", [128, 2048], F32))
    return B


def load_consts(em, B, d, recip=False):
    for name, dst in (("ones_f", B.ones_f), ("ones_b", B.ones_b),
                      ("ident", B.ident), ("sel9", B.sel9), ("sel8", B.sel8),
                      ("triu", B.triu)):
        em.dma("sync", dst[:], d[name][:])
    if recip:
        em.dma("sync", B.recip[:], d["recip64"][:])
    return em.cnt["ld"]


def emit_ln(em, B, x_in, tt, xtt_free=(), ps_wait=()):
    x3 = B.xtt.rearrange("p (c t) -> p c t", c=8)
    src = x_in.rearrange("(c p) t -> p c t", p=128)[:, :, tt * TT:(tt + 1) * TT]
    vx = em.dma("sync", x3[:], src, waits=tuple(xtt_free))

    for fc in range(NFC):
        xc = B.xtt[:, fc * TT:(fc + 1) * TT]
        for hh in range(2):
            mm_op(em, B.psA[0:1, hh * 512:(hh + 1) * 512],
                  B.ones_f[:, 0:1], xc[:, hh * 512:(hh + 1) * 512],
                  start=(fc == 0), stop=(fc == NFC - 1),
                  waits=(("ld", vx),) + tuple(ps_wait))
    vsq_mm = None
    for fc in range(NFC):
        xc = B.xtt[:, fc * TT:(fc + 1) * TT]
        w = [("ld", vx)]
        if vsq_mm is not None:
            w.append(("pe", vsq_mm))
        vdv = tt_op(em, B.sq[:], xc, xc, ALU.mult, waits=tuple(w))
        for hh in range(2):
            vsq_mm = mm_op(em, B.psA[0:1, 1024 + hh * 512:1024 + (hh + 1) * 512],
                           B.ones_f[:, 0:1], B.sq[:, hh * 512:(hh + 1) * 512],
                           start=(fc == 0), stop=(fc == NFC - 1),
                           waits=(("dv", vdv),))
    mean = B.rowsA[:, 0:TT]
    rstd = B.rowsA[:, TT:2 * TT]
    aux = B.sq[0:1, 0:TT]
    u0 = ts_op(em, mean, B.psA[0:1, 0:1024], 1.0 / D, ALU.mult,
               waits=(("pe", vsq_mm),))
    u1 = ts_op(em, aux, B.psA[0:1, 1024:2048], 1.0 / D, ALU.mult)
    u2 = tt_op(em, rstd, mean, mean, ALU.mult, waits=(("dv", u0),))
    u3 = tt_op(em, aux, aux, rstd, ALU.subtract, waits=(("dv", u2),))
    u4 = ts_op(em, aux, aux, EPS, ALU.add, waits=(("dv", u3),))
    v1 = rc_op(em, aux, aux, waits=(("dv", u4),))
    va = act_op(em, rstd, aux, AF.Sqrt, waits=(("dv", v1),))
    vbc = None
    for hh in range(2):
        mm_op(em, B.psB[:, hh * 512:(hh + 1) * 512],
              B.ones_f[0:1, :], mean[:, hh * 512:(hh + 1) * 512],
              start=True, stop=True, waits=(("dv", v1),))
        vbc = mm_op(em, B.psB[:, 1024 + hh * 512:1024 + (hh + 1) * 512],
                    B.ones_f[0:1, :], rstd[:, hh * 512:(hh + 1) * 512],
                    start=True, stop=True, waits=(("ac", va),))
    vh = None
    for fc in range(NFC):
        xc = B.xtt[:, fc * TT:(fc + 1) * TT]
        hc = B.h[:, fc * TT:(fc + 1) * TT]
        tt_op(em, B.sq[:], xc, B.psB[:, 0:1024], ALU.subtract,
              waits=(("pe", vbc),))
        vh = tt_op(em, hc, B.sq[:], B.psB[:, 1024:2048], ALU.mult)
    return vh, vx


def emit_router(em, B, nE, tt, vh, wr_ready, sbc_ready, vx):
    v_free = None
    v_trc = None
    vsig_in = None
    for tc in range(8):
        lg_ps = B.psA[:, 0:nE]
        for fc in range(NFC):
            w = [("dv", vh), ("ld", max(wr_ready, vx)), ("dv", sbc_ready)]
            if v_free is not None:
                w.append(("dv", v_free))
            vmm = mm_op(em, lg_ps,
                        B.xtt[:, fc * TT + tc * 128: fc * TT + tc * 128 + 128],
                        B.wrf[:, fc * 10: fc * 10 + nE],
                        start=(fc == 0), stop=(fc == NFC - 1),
                        waits=tuple(w) if fc == 0 else ())
        w = [("dv", vh)]
        if v_trc is not None:
            w.append(("dv", v_trc))
        vt1 = tr_op(em, B.psA[0:128, 512:513],
                    B.rowsA[0:1, tc * 128:(tc + 1) * 128], B.ident[0:1, 0:1],
                    waits=tuple(w))
        vt2 = tr_op(em, B.psA[0:128, 513:514],
                    B.rowsA[0:1, TT + tc * 128: TT + (tc + 1) * 128],
                    B.ident[0:1, 0:1])
        mcol = B.lg[:, 112:113]
        rcol = B.lg[:, 113:114]
        v0 = cp_op(em, mcol, B.psA[0:128, 512:513], waits=(("pe", vt2),))
        v_trc = cp_op(em, rcol, B.psA[0:128, 513:514])
        u1 = B.lg[:, 16:16 + nE]
        lgf = B.lg[:, 0:nE]
        v1 = ts_op(em, u1, B.sbc[:, 0:nE], mcol, ALU.mult, waits=(("dv", v0),))
        v_free = tt_op(em, lgf, lg_ps, u1, ALU.subtract,
                       waits=(("pe", vmm), ("dv", v1)))
        v2 = ts_op(em, lgf, lgf, rcol, ALU.mult, waits=(("dv", v_trc),))
        d = B.lg[:, 32:32 + nE]
        ind1 = B.sq[:, tc * 64: tc * 64 + nE]
        ind2 = B.sq[:, tc * 64 + 32: tc * 64 + 32 + nE]
        msk = B.lg[:, 48:48 + nE]
        m1 = B.lg[:, 64:65]
        v3 = red_op(em, m1, lgf, waits=(("dv", v2),))
        v4 = ts_op(em, d, lgf, m1, ALU.subtract, waits=(("dv", v3),))
        v5 = ts_op(em, ind1, d, 0.0, ALU.is_equal, waits=(("dv", v4),))
        v6 = ts_op(em, msk, ind1, -1e30, ALU.mult, waits=(("dv", v5),))
        v7 = tt_op(em, msk, msk, d, ALU.add, waits=(("dv", v6),))
        m2 = B.m2s[:, tc:tc + 1]
        v8 = red_op(em, m2, msk, waits=(("dv", v7),))
        vsig_in = ts_op(em, ind2, msk, m2, ALU.is_equal, waits=(("dv", v8),))
    w2 = B.m2s[:, 8:16]
    w1 = B.m2s[:, 16:24]
    va = act_op(em, w2, B.m2s[:, 0:8], AF.Sigmoid, waits=(("dv", vsig_in),))
    v9 = ts_op(em, w1, w2, -0.5, ALU.mult, waits=(("ac", va),))
    v10 = ts_op(em, w1, w1, 0.5, ALU.add, waits=(("dv", v9),))
    vw = ts_op(em, w2, w2, 0.5, ALU.mult, waits=(("dv", v10),))
    vtr = None
    vb = None
    for tc in range(8):
        ind1 = B.sq[:, tc * 64: tc * 64 + nE]
        ind2 = B.sq[:, tc * 64 + 32: tc * 64 + 32 + nE]
        cwa = B.cwtc[:, (tc % 2) * 16: (tc % 2) * 16 + nE]
        scr = B.lg[:, 96:96 + nE]
        w = [("dv", vw)]
        if vtr is not None and tc >= 2:
            w.append(("pe", vtr))
        va1 = ts_op(em, cwa, ind1, B.m2s[:, 16 + tc:17 + tc], ALU.mult, waits=tuple(w))
        va2 = ts_op(em, scr, ind2, B.m2s[:, 8 + tc:9 + tc], ALU.mult, waits=(("dv", va1),))
        vcw = tt_op(em, cwa, cwa, scr, ALU.add, waits=(("dv", va2),))
        vtr = tr_op(em, B.psA[0:nE, 1024:1152], cwa, B.ident[:], waits=(("dv", vcw),))
        vb = cp_op(em, B.cw9b[0:nE, tt * TT + tc * 128: tt * TT + tc * 128 + 128],
                   B.psA[0:nE, 1024:1152], waits=(("pe", vtr),))
    return vb


def emit_experts(em, B, tt, nE, n_ic, wg_d, wu_d, wd_d, vh, vcw, vx_prev,
                 scr=None, mode="cast_store"):
    K_IC = 1024
    stage_slot = [B.wstage[:, 0:1024], B.wstage[:, 1024:2048]]
    wgu_slot = [B.wgu[:, i * 1024:(i + 1) * 1024] for i in range(4)]
    half = n_ic // 2 * 128
    wd_slot = [B.wd[:, 0:n_ic * 128], B.wd[:, n_ic * 128:2 * n_ic * 128]]
    stage_free = [None, None]
    wgu_free = [None] * 4
    wdst_free = [None, None]
    wd_free = [None, None]
    sg_free = {}
    ps_free = {}
    vx_last = vx_prev
    sel = B.sel9 if nE == 9 else B.sel8

    for e in range(nE):
        w = [("dv", vcw)]
        if vx_last is not None:
            w.append(("dv", vx_last))
        if "cwb" in ps_free:
            w.append(ps_free["cwb"])
        vmm = None
        for hh in range(2):
            vmm = mm_op(em, B.psB[:, hh * 512:(hh + 1) * 512],
                        sel[0:nE, e * 128:(e + 1) * 128],
                        B.cw9b[0:nE, tt * TT + hh * 512: tt * TT + (hh + 1) * 512],
                        start=True, stop=True, waits=tuple(w) if hh == 0 else ())
        vcwb = cp_op(em, B.cwb[:], B.psB[:, 0:1024], waits=(("pe", vmm),))
        ps_free["cwb"] = ("dv", vcwb)
        vhe = None
        for fc in range(NFC):
            vhe = tt_op(em, B.he[:, fc * TT:(fc + 1) * TT],
                        B.h[:, fc * TT:(fc + 1) * TT], B.cwb[:], ALU.mult,
                        waits=(("dv", vh), ("dv", vcwb)))
        vhid_last = None
        for ic in range(n_ic):
            for mat, wsrc, rhs_buf in ((0, wg_d, B.h), (1, wu_d, B.he)):
                s = mat
                b = mat * 2 + (ic % 2)
                off = (e * n_ic + ic) * K_IC
                wfree = list(wgu_free[b]) if wgu_free[b] is not None else []
                if mode == "bf16":
                    vld = em.dma("sync", wgu_slot[b][:],
                                 scr[mat][:, off: off + K_IC], waits=tuple(wfree))
                    ready = ("ld", vld)
                    vcast = None
                else:
                    w = []
                    if stage_free[s] is not None:
                        w.append(stage_free[s])
                    vld = em.dma("sync", stage_slot[s][:],
                                 wsrc[:, off: off + K_IC], waits=tuple(w))
                    w2 = [("ld", vld)] + wfree
                    vcast = cp_op(em, wgu_slot[b][:], stage_slot[s][:],
                                  waits=tuple(w2), eng="gpsimd", sem="gp")
                    stage_free[s] = ("gp", vcast)
                    ready = ("gp", vcast)
                ps = B.psA if (ic % 2 == 0) else B.psB
                base = 0 if mat == 0 else 1024
                for hh in range(2):
                    key = ("g" if mat == 0 else "u", ic % 2, hh)
                    wm = [ready, ("dv", vhe if mat == 1 else vh)]
                    if key in ps_free:
                        wm.append(ps_free[key])
                    for fc in range(NFC):
                        vmm = mm_op(em, ps[:, base + hh * 512:base + (hh + 1) * 512],
                                    wgu_slot[b][:, fc * 128:(fc + 1) * 128],
                                    rhs_buf[:, fc * TT + hh * 512:
                                            fc * TT + (hh + 1) * 512],
                                    start=(fc == 0), stop=(fc == NFC - 1),
                                    waits=tuple(wm) if fc == 0 else ())
                    if mat == 0:
                        skey = (ic % 2, hh)
                        wsl = [("pe", vmm)]
                        if skey in sg_free:
                            wsl.append(sg_free[skey])
                        vsl = act_op(em, B.sg[:, (ic % 2) * TT + hh * 512:
                                              (ic % 2) * TT + (hh + 1) * 512],
                                     ps[:, hh * 512:(hh + 1) * 512], AF.Silu,
                                     waits=tuple(wsl))
                        ps_free[key] = ("ac", vsl)
                    else:
                        vsl = ps_free[("g", ic % 2, hh)][1]
                        vhd = tt_op(em, B.big[:, ic * TT + hh * 512:
                                           ic * TT + (hh + 1) * 512],
                                    B.sg[:, (ic % 2) * TT + hh * 512:
                                         (ic % 2) * TT + (hh + 1) * 512],
                                    ps[:, 1024 + hh * 512:1024 + (hh + 1) * 512],
                                    ALU.mult,
                                    waits=(("pe", vmm), ("ac", vsl)))
                        ps_free[key] = ("dv", vhd)
                        ps_free[("g", ic % 2, hh)] = ("dv", vhd)
                        sg_free[(ic % 2, hh)] = ("dv", vhd)
                        vhid_last = vhd
                wgu_free[b] = [("pe", vmm)]
                if mode == "cast_store":
                    vst = em.op("gpsimd",
                                lambda e, o=scr[mat][:, off: off + K_IC],
                                i=wgu_slot[b][:]: e.dma_start(out=o, in_=i),
                                (("gp", vcast),), "st", 16)
                    wgu_free[b].append(("st", vst))
        for oc in range(8):
            s = oc % 2
            off = (e * 8 + oc) * n_ic * 128
            wdfree = list(wd_free[s]) if wd_free[s] is not None else []
            if mode == "bf16":
                vld = None
                for hf in range(2):
                    vld = em.dma("sync", wd_slot[s][:, hf * half:(hf + 1) * half],
                                 scr[2][:, off + hf * half: off + (hf + 1) * half],
                                 waits=tuple(wdfree) if hf == 0 else ())
                ready = ("ld", vld)
                vcast = None
            else:
                vcast = None
                for hf in range(2):
                    w = []
                    if wdst_free[hf] is not None:
                        w.append(wdst_free[hf])
                    vld = em.dma("sync", B.wdstage[:, hf * half:(hf + 1) * half],
                                 wd_d[:, off + hf * half: off + (hf + 1) * half],
                                 waits=tuple(w))
                    w2 = [("ld", vld)] + (wdfree if hf == 0 else [])
                    vcast = cp_op(em, wd_slot[s][:, hf * half:(hf + 1) * half],
                                  B.wdstage[:, hf * half:(hf + 1) * half],
                                  waits=tuple(w2), eng="gpsimd", sem="gp")
                    wdst_free[hf] = ("gp", vcast)
                ready = ("gp", vcast)
            ps = B.psA if (oc % 2 == 0) else B.psB
            for hh in range(2):
                key = ("d", oc % 2, hh)
                wm = [ready, ("dv", vhid_last)]
                if key in ps_free:
                    wm.append(ps_free[key])
                for ic in range(n_ic):
                    vmm = mm_op(em, ps[:, hh * 512:(hh + 1) * 512],
                                wd_slot[s][:, ic * 128:(ic + 1) * 128],
                                B.big[:, ic * TT + hh * 512: ic * TT + (hh + 1) * 512],
                                start=(ic == 0), stop=(ic == n_ic - 1),
                                waits=tuple(wm) if ic == 0 else ())
                xs = B.xtt[:, oc * TT + hh * 512: oc * TT + (hh + 1) * 512]
                vx_last = tt_op(em, xs, xs, ps[:, hh * 512:(hh + 1) * 512], ALU.add,
                                waits=(("pe", vmm),))
                ps_free[key] = ("dv", vx_last)
            wd_free[s] = [("pe", vmm)]
            if mode == "cast_store":
                vst = em.op("gpsimd",
                            lambda e, o=scr[2][:, off: off + n_ic * 128],
                            i=wd_slot[s][:, 0:n_ic * 128]: e.dma_start(out=o, in_=i),
                            (("gp", vcast),), "st", 16)
                wd_free[s].append(("st", vst))
    return vx_last


def emit_moe_phase(em, B, d, nE, n_ic, pre, first=True):
    x_in, x_out = d["xin"], d["xout"]
    wr_d, wg_d, wu_d, wd_d = d[pre + "wr"], d[pre + "wg"], d[pre + "wu"], d[pre + "wd"]
    scr = (d[pre + "wgb"], d[pre + "wub"], d[pre + "wdb"])
    wr_ready = em.dma("sync", B.wrf[:], wr_d[:])
    vs = None
    for fc in range(NFC):
        vs = mm_op(em, B.psA[0:1, 0:nE], B.ones_f[:, 0:1],
                   B.wrf[:, fc * 10: fc * 10 + nE],
                   start=(fc == 0), stop=(fc == NFC - 1),
                   waits=(("ld", wr_ready),))
    vsr = cp_op(em, B.srow[0:1, 0:nE], B.psA[0:1, 0:nE], waits=(("pe", vs),))
    vsb = mm_op(em, B.psA[:, 0:nE], B.ones_f[0:1, :], B.srow[0:1, 0:nE],
                start=True, stop=True, waits=(("dv", vsr),))
    sbc_ready = cp_op(em, B.sbc[:, 0:nE], B.psA[:, 0:nE], waits=(("pe", vsb),))
    vstore = None
    for tt in range(NTT):
        xfree = [("st", vstore)] if vstore is not None else []
        psw = [("dv", sbc_ready)] if tt == 0 else []
        vh, vx = emit_ln(em, B, x_in, tt, xtt_free=xfree, ps_wait=psw)
        vcw = emit_router(em, B, nE, tt, vh, wr_ready, sbc_ready, vx)
        mode = "cast_store" if (first and tt == 0) else "bf16"
        vx_last = emit_experts(em, B, tt, nE, n_ic, wg_d, wu_d, wd_d, vh, vcw,
                               None, scr=scr, mode=mode)
        x3 = B.xtt.rearrange("p (c t) -> p c t", c=8)
        dst = x_out.rearrange("(c p) t -> p c t", p=128)[:, :, tt * TT:(tt + 1) * TT]
        vstore = em.op("gpsimd", lambda e, o=dst, i=x3[:]: e.dma_start(out=o, in_=i),
                       (("dv", vx_last),), "st", 16)
    em.barrier()


def emit_qkv_phase(em, B, d, qscale, sign_kv):
    x_in, wq_d = d["xin"], d["wq"]
    q_fm, k_fm, v_tm = d["qfm"], d["kfm"], d["vtm"]
    k_tm = d.get("ktm")
    vcast_all = None
    stage_free = [None, None]
    for sl in range(24):
        s = sl % 2
        w = []
        if stage_free[s] is not None:
            w.append(stage_free[s])
        vld = em.dma("sync", B.wstage[:, s * 1024:(s + 1) * 1024],
                     wq_d[:, sl * 1024:(sl + 1) * 1024], waits=tuple(w))
        vcast_all = cp_op(em, B.big[:, sl * 1024:(sl + 1) * 1024],
                          B.wstage[:, s * 1024:(s + 1) * 1024],
                          waits=(("ld", vld),), eng="gpsimd", sem="gp")
        stage_free[s] = ("gp", vcast_all)
    he2 = B.he
    slot_free = {}
    ps_free = {}
    vh = None
    va = None
    for tt in range(NTT):
        xfree = (("dv", vh),) if vh is not None else ()
        pswait = (("ac", va),) if va is not None else ()
        vh, vx = emit_ln(em, B, x_in, tt, xtt_free=xfree, ps_wait=pswait)
        for jc in range(16):
            ps = B.psA if jc % 2 == 0 else B.psB
            key = ("qk", jc % 2)
            for hh in range(2):
                wm = [("dv", vh), ("gp", vcast_all)]
                if key in ps_free:
                    wm.append(ps_free[key])
                for fc in range(NFC):
                    vmm = mm_op(em, ps[:, hh * 512:(hh + 1) * 512],
                                B.big[:, jc * 1024 + fc * 128: jc * 1024 + fc * 128 + 128],
                                B.h[:, fc * TT + hh * 512: fc * TT + (hh + 1) * 512],
                                start=(fc == 0), stop=(fc == NFC - 1),
                                waits=tuple(wm) if fc == 0 else ())
            sl = jc % 4
            wv = [("pe", vmm)]
            if sl in slot_free:
                wv.append(slot_free[sl])
            dst_sb = he2[:, sl * 1024:(sl + 1) * 1024]
            if jc < 8:
                va = act_op(em, dst_sb, ps[:, 0:1024], AF.Copy,
                            scale=qscale, waits=tuple(wv))
            elif sign_kv:
                va = act_op(em, dst_sb, ps[:, 0:1024], AF.Sign, waits=tuple(wv))
            else:
                va = act_op(em, dst_sb, ps[:, 0:1024], AF.Copy, waits=tuple(wv))
            ps_free[key] = ("ac", va)
            dram = q_fm if jc < 8 else k_fm
            r0 = (jc % 8) * 128
            vst = em.op("gpsimd",
                        lambda e, o=dram[r0:r0 + 128, tt * TT:(tt + 1) * TT], i=dst_sb:
                        e.dma_start(out=o, in_=i),
                        (("ac", va),), "st", 16)
            slot_free[sl] = ("st", vst)
        tok_outs = [("v", v_tm, 16)]
        if k_tm is not None:
            tok_outs = [("k", k_tm, 8), ("v", v_tm, 16)]
        for name, dram, jbase in tok_outs:
            for tc in range(8):
                ps = B.psA if tc % 2 == 0 else B.psB
                key = ("tok" + name, tc % 2)
                wm = [("dv", vh), ("gp", vcast_all)]
                if key in ps_free:
                    wm.append(ps_free[key])
                for jc in range(8):
                    for fc in range(NFC):
                        vmm = mm_op(em, ps[:, 1024 + jc * 128:1024 + (jc + 1) * 128],
                                    B.h[:, fc * TT + tc * 128: fc * TT + tc * 128 + 128],
                                    B.big[:, (jbase + jc) * 1024 + fc * 128:
                                         (jbase + jc) * 1024 + fc * 128 + 128],
                                    start=(fc == 0), stop=(fc == NFC - 1),
                                    waits=tuple(wm) if (fc == 0 and jc == 0) else ())
                sl = 4 + tc % 2 + (2 if name == "v" else 0)
                wv = [("pe", vmm)]
                if sl in slot_free:
                    wv.append(slot_free[sl])
                dst_sb = he2[:, sl * 1024:(sl + 1) * 1024]
                if sign_kv:
                    va = act_op(em, dst_sb, ps[:, 1024:2048], AF.Sign, waits=tuple(wv))
                else:
                    va = act_op(em, dst_sb, ps[:, 1024:2048], AF.Copy, waits=tuple(wv))
                ps_free[key] = ("ac", va)
                r0 = tt * TT + tc * 128
                vst = em.op("gpsimd",
                            lambda e, o=dram[r0:r0 + 128, :], i=dst_sb:
                            e.dma_start(out=o, in_=i),
                            (("ac", va),), "st", 16)
                slot_free[sl] = ("st", vst)
    em.barrier()


def emit_attn0_phase(em, B, d):
    x_in, x_out = d["xin"], d["xout"]
    q_fm, k_fm, v_tm = d["qfm"], d["kfm"], d["vtm"]
    vt3 = B.vt.rearrange("p (c q) -> p c q", q=65)
    vones = ms_op(em, vt3[:, :, 64:65], 1.0)
    qk_free = None
    ps_free = {}
    buf_free = {}
    vstore = None
    for hd in range(16):
        w = [("dv", vones)] if hd == 0 else [("pe", qk_free)]
        vq = em.dma("sync", B.qb[0:64, :], q_fm[hd * 64:(hd + 1) * 64, :],
                    waits=tuple(w))
        vk = em.dma("sync", B.kb[0:64, :], k_fm[hd * 64:(hd + 1) * 64, :])
        vv = em.dma("sync", vt3[:, :, 0:64],
                    v_tm.rearrange("(c p) d -> p c d", p=128)[:, :, hd * 64:(hd + 1) * 64])
        for qt in range(8):
            smm = {}
            exps = {}

            def sreg(par):
                if par < 2:
                    return B.psA[:, par * 1024:(par + 1) * 1024]
                return B.psB[:, 1024:2048]

            def S(pair, vq=vq, vk=vk, qt=qt):
                par = pair % 3
                ps_s = sreg(par)
                w1 = [("ld", vq), ("ld", vk)]
                if ("s", par) in ps_free:
                    w1.append(ps_free[("s", par)])
                v = None
                for kc2 in range(2):
                    kc = pair * 2 + kc2
                    v = mm_op(em, ps_s[:, kc2 * 512:(kc2 + 1) * 512],
                              B.kb[0:64, kc * 128:(kc + 1) * 128],
                              B.qb[0:64, qt * 512:(qt + 1) * 512],
                              start=True, stop=True,
                              waits=tuple(w1) if kc2 == 0 else ())
                smm[pair] = v

            def E(pair):
                par = pair % 3
                ps_s = sreg(par)
                pr = B.probs[:, par * 1024:(par + 1) * 1024]
                w2 = [("pe", smm[pair])]
                if ("p", par) in buf_free:
                    w2.append(buf_free[("p", par)])
                va = act_op(em, pr, ps_s, AF.Exp, waits=tuple(w2))
                ps_free[("s", par)] = ("ac", va)
                exps[pair] = va

            def PV(pair, vv=vv):
                par = pair % 3
                pr = B.probs[:, par * 1024:(par + 1) * 1024]
                v = None
                for kc2 in range(2):
                    kc = pair * 2 + kc2
                    w3 = [("ac", exps[pair]), ("ld", vv)]
                    if pair == 0 and kc2 == 0 and "o" in ps_free:
                        w3.append(ps_free["o"])
                    v = mm_op(em, B.psB[0:65, 0:512],
                              B.vt[:, kc * 65: kc * 65 + 65],
                              pr[:, kc2 * 512:(kc2 + 1) * 512],
                              start=(pair == 0 and kc2 == 0),
                              stop=(pair == 15 and kc2 == 1),
                              waits=tuple(w3) if kc2 == 0 else ())
                buf_free[("p", par)] = ("pe", v)
                return v

            S(0)
            E(0)
            S(1)
            E(1)
            S(2)
            E(2)
            for pair in range(13):
                PV(pair)
                S(pair + 3)
                E(pair + 3)
            PV(13)
            PV(14)
            vpv = PV(15)
            vr = rc_op(em, B.zr[0:1, 0:512], B.psB[64:65, 0:512],
                       waits=(("pe", vpv),))
            vrb = cp_op(em, B.zrb[0:1, 0:512], B.zr[0:1, 0:512],
                        waits=(("dv", vr),))
            w4 = [("dv", vrb)]
            if "zb" in ps_free:
                w4.append(ps_free["zb"])
            vzm = mm_op(em, B.psB[0:64, 512:1024], B.ones_b[0:1, 0:64],
                        B.zrb[0:1, 0:512], start=True, stop=True, waits=tuple(w4))
            vzb = cp_op(em, B.zb[:, 0:512], B.psB[0:64, 512:1024],
                        waits=(("pe", vzm),))
            ps_free["zb"] = ("dv", vzb)
            vtm_ = tt_op(em, B.tmp64[:, 0:512], B.psB[0:64, 0:512], B.zb[:, 0:512],
                         ALU.mult, waits=(("dv", vzb),))
            ps_free["o"] = ("dv", vtm_)
            w5 = []
            if "xsl" in buf_free:
                w5.append(buf_free["xsl"])
            vxl = em.dma("sync", B.xsl[:, 0:512],
                         x_in[hd * 64:(hd + 1) * 64, qt * 512:(qt + 1) * 512],
                         waits=tuple(w5))
            vxa = tt_op(em, B.xsl[:, 0:512], B.xsl[:, 0:512], B.tmp64[:, 0:512],
                        ALU.add, waits=(("ld", vxl), ("dv", vtm_)))
            vstore = em.op("gpsimd",
                           lambda e, o=x_out[hd * 64:(hd + 1) * 64,
                                             qt * 512:(qt + 1) * 512],
                           i=B.xsl[:, 0:512]: e.dma_start(out=o, in_=i),
                           (("dv", vxa),), "st", 16)
            buf_free["xsl"] = ("st", vstore)
        qk_free = vpv
    em.barrier()


def emit_attn1_phase(em, B, d):
    q_fm, k_fm, k_tm, v_tm = d["qfm"], d["kfm"], d["ktm"], d["vtm"]
    ctx_d = d["ctx"]
    ctx_sb = B.h[0:64, 0:4096]

    def kchunk(n):
        return B.he[0:64, n * 64:(n + 1) * 64]

    def vchunk(n):
        return B.he[0:64, 4096 + n * 64: 4096 + (n + 1) * 64]
    buf_free = {}
    ps_free = {}
    last_mm = None
    for hd in range(16):
        w = [] if last_mm is None else [("pe", last_mm)]
        vq = em.dma("sync", B.qb[0:64, :], q_fm[hd * 64:(hd + 1) * 64, :],
                    waits=tuple(w))
        vk = em.dma("sync", B.kb[0:64, :], k_fm[hd * 64:(hd + 1) * 64, :])
        vkt = em.dma("sync",
                     B.he[0:64, 0:4096].rearrange("p (c j) -> p c j", j=64),
                     k_tm.rearrange("(c p) d -> p c d", p=64)[:, :, hd * 64:(hd + 1) * 64],
                     waits=tuple(w))
        vvt = em.dma("sync",
                     B.he[0:64, 4096:8192].rearrange("p (c j) -> p c j", j=64),
                     v_tm.rearrange("(c p) d -> p c d", p=64)[:, :, hd * 64:(hd + 1) * 64])
        wc = [] if last_mm is None else [("pe", last_mm)]
        vpb = ms_op(em, B.pbf[:], 0.0, waits=tuple(wc))
        vctx = None
        for n in range(64):
            par = n % 2
            ps_sT = B.psA[0:64, par * 512: par * 512 + 64]
            w1 = [("ld", vq), ("ld", vk)]
            if ("sT", par) in ps_free:
                w1.append(ps_free[("sT", par)])
            vs = mm_op(em, ps_sT, B.kb[0:64, n * 64:(n + 1) * 64],
                       B.qb[0:64, n * 64:(n + 1) * 64], start=True, stop=True,
                       waits=tuple(w1))
            sm = B.smT[:, par * 64: par * 64 + 64]
            w2 = [("pe", vs)]
            if ("sm", par) in buf_free:
                w2.append(buf_free[("sm", par)])
            vsm = tt_op(em, sm, ps_sT, B.triu[:], ALU.mult, waits=tuple(w2))
            ps_free[("sT", par)] = ("dv", vsm)
            vpe = cp_op(em, B.pbf[:], B.psB[0:64, 0:64],
                        waits=(("pe", last_mm),)) if n > 0 else vpb
            ps_o = B.psA[0:64, 1024 + par * 512: 1024 + par * 512 + 64]
            w3 = [("ld", vkt), ("ld", vvt), ("dv", vsm)]
            if ("o", par) in ps_free:
                w3.append(ps_free[("o", par)])
            mm_op(em, ps_o, vchunk(n), sm, start=True, stop=False,
                  waits=tuple(w3))
            vo = mm_op(em, ps_o, B.pbf[:], B.qb[0:64, n * 64:(n + 1) * 64],
                       start=False, stop=True, waits=(("dv", vpe),))
            last_mm = mm_op(em, B.psB[0:64, 0:64], kchunk(n), vchunk(n),
                            start=(n == 0), stop=(n == 63),
                            waits=(("dv", vpe),))
            vctx = tt_op(em, ctx_sb[:, n * 64:(n + 1) * 64], ps_o,
                         B.recip[:, n * 64:(n + 1) * 64], ALU.mult,
                         waits=(("pe", vo),) + ((buf_free["ctx"],) if n == 0 and "ctx" in buf_free else ()))
            ps_free[("o", par)] = ("dv", vctx)
        vst = em.op("gpsimd",
                    lambda e, o=ctx_d[hd * 64:(hd + 1) * 64, :], i=ctx_sb:
                    e.dma_start(out=o, in_=i), (("dv", vctx),), "st", 16)
        buf_free["ctx"] = ("st", vst)
    em.barrier()


def emit_proj_phase(em, B, d):
    x_in, x_out, ctx_d, wo_d = d["xin"], d["xout"], d["ctx"], d["wo"]
    stage_free = [None, None]
    vcast_all = None
    for sl in range(8):
        s = sl % 2
        w = []
        if stage_free[s] is not None:
            w.append(stage_free[s])
        vld = em.dma("sync", B.wstage[:, s * 1024:(s + 1) * 1024],
                     wo_d[:, sl * 1024:(sl + 1) * 1024], waits=tuple(w))
        vcast_all = cp_op(em, B.big[:, sl * 1024:(sl + 1) * 1024],
                          B.wstage[:, s * 1024:(s + 1) * 1024],
                          waits=(("ld", vld),), eng="gpsimd", sem="gp")
        stage_free[s] = ("gp", vcast_all)
    vstore = None
    ps_free = {}
    for tt in range(NTT):
        x3 = B.xtt.rearrange("p (c t) -> p c t", c=8)
        xfree = [("st", vstore)] if vstore is not None else []
        vx = em.dma("sync", x3[:],
                    x_in.rearrange("(c p) t -> p c t", p=128)[:, :, tt * TT:(tt + 1) * TT],
                    waits=tuple(xfree))
        c3 = B.h.rearrange("p (c t) -> p c t", c=8)
        vc = em.dma("sync", c3[:],
                    ctx_d.rearrange("(c p) t -> p c t", p=128)[:, :, tt * TT:(tt + 1) * TT],
                    waits=tuple(xfree))
        vx_last = None
        for oc in range(8):
            ps = B.psA if oc % 2 == 0 else B.psB
            for hh in range(2):
                key = (oc % 2, hh)
                wm = [("ld", vc), ("gp", vcast_all)]
                if key in ps_free:
                    wm.append(ps_free[key])
                for dc in range(NFC):
                    vmm = mm_op(em, ps[:, hh * 512:(hh + 1) * 512],
                                B.big[:, oc * 1024 + dc * 128: oc * 1024 + dc * 128 + 128],
                                B.h[:, dc * TT + hh * 512: dc * TT + (hh + 1) * 512],
                                start=(dc == 0), stop=(dc == NFC - 1),
                                waits=tuple(wm) if dc == 0 else ())
                xs = B.xtt[:, oc * TT + hh * 512: oc * TT + (hh + 1) * 512]
                vx_last = tt_op(em, xs, xs, ps[:, hh * 512:(hh + 1) * 512], ALU.add,
                                waits=(("pe", vmm), ("ld", vx)))
                ps_free[key] = ("dv", vx_last)
        dst = x_out.rearrange("(c p) t -> p c t", p=128)[:, :, tt * TT:(tt + 1) * TT]
        vstore = em.op("gpsimd", lambda e, o=dst, i=x3[:]: e.dma_start(out=o, in_=i),
                       (("dv", vx_last),), "st", 16)
    em.barrier()


CONST_SPECS = [
    ("ones_f", [128, 128], F32), ("ones_b", [128, 128], BF16),
    ("ident", [128, 128], F32), ("sel9", [9, 9 * 128], BF16),
    ("sel8", [8, 8 * 128], BF16), ("triu", [64, 64], F32),
]

WEIGHT_SPECS = [
    ("m0wr", [128, 80], F32),
    ("m0wg", [128, 9 * 22 * 1024], F32),
    ("m0wu", [128, 9 * 22 * 1024], F32),
    ("m0wd", [128, 9 * 8 * 22 * 128], F32),
    ("wq0", [128, 24 * 1024], F32),
    ("m1wr", [128, 80], F32),
    ("m1wg", [128, 8 * 8 * 1024], F32),
    ("m1wu", [128, 8 * 8 * 1024], F32),
    ("m1wd", [128, 8 * 8 * 8 * 128], F32),
    ("wq1", [128, 24 * 1024], F32),
    ("wo1", [128, 8 * 1024], F32),
]


LAUNCH_PLAN = [
    ("moe0", {"xin": "X", "m0wr": "m0wr", "m0wg": "m0wg", "m0wu": "m0wu",
              "m0wd": "m0wd"}, {"xout": "X"}),
    ("attn0a", {"xin": "X", "wq": "wq0"}, {"qfm": "qfm", "kfm": "kfm",
                                           "vtm": "vtm"}),
    ("attn0b", {"xin": "X", "qfm": "qfm", "kfm": "kfm", "vtm": "vtm"},
     {"xout": "X"}),
    ("moe0", {"xin": "X", "m0wr": "m0wr", "m0wg": "m0wg", "m0wu": "m0wu",
              "m0wd": "m0wd"}, {"xout": "X"}),
    ("moe1", {"xin": "X", "m1wr": "m1wr", "m1wg": "m1wg", "m1wu": "m1wu",
              "m1wd": "m1wd"}, {"xout": "X"}),
    ("attn1a", {"xin": "X", "wq": "wq1"}, {"qfm": "qfm", "kfm": "kfm",
                                           "ktm": "ktm", "vtm": "vtm"}),
    ("attn1b", {"qfm": "qfm", "kfm": "kfm", "ktm": "ktm", "vtm": "vtm",
                "recip64": "recip64"}, {"ctx": "ctx"}),
    ("attn1c", {"xin": "X", "ctx": "ctx", "wo": "wo1"}, {"xout": "X"}),
    ("moe1", {"xin": "X", "m1wr": "m1wr", "m1wg": "m1wg", "m1wu": "m1wu",
              "m1wd": "m1wd"}, {"xout": "X"}),
]

PHASE_IO = {
    "moe0":   dict(ins=[("xin", [D, T], F32), ("m0wr", [128, 80], F32),
                        ("m0wg", [128, 9 * 22 * 1024], F32),
                        ("m0wu", [128, 9 * 22 * 1024], F32),
                        ("m0wd", [128, 9 * 8 * 22 * 128], F32)],
                   outs=[("xout", [D, T], F32)],
                   internal=[("m0wgb", [128, 9 * 22 * 1024], BF16),
                             ("m0wub", [128, 9 * 22 * 1024], BF16),
                             ("m0wdb", [128, 9 * 8 * 22 * 128], BF16)]),
    "attn0a": dict(ins=[("xin", [D, T], F32), ("wq", [128, 24 * 1024], F32)],
                   outs=[("qfm", [D, T], BF16), ("kfm", [D, T], BF16),
                         ("vtm", [T, D], BF16)]),
    "attn0b": dict(ins=[("xin", [D, T], F32), ("qfm", [D, T], BF16),
                        ("kfm", [D, T], BF16), ("vtm", [T, D], BF16)],
                   outs=[("xout", [D, T], F32)]),
    "moe1":   dict(ins=[("xin", [D, T], F32), ("m1wr", [128, 80], F32),
                        ("m1wg", [128, 8 * 8 * 1024], F32),
                        ("m1wu", [128, 8 * 8 * 1024], F32),
                        ("m1wd", [128, 8 * 8 * 8 * 128], F32)],
                   outs=[("xout", [D, T], F32)],
                   internal=[("m1wgb", [128, 8 * 8 * 1024], BF16),
                             ("m1wub", [128, 8 * 8 * 1024], BF16),
                             ("m1wdb", [128, 8 * 8 * 8 * 128], BF16)]),
    "attn1a": dict(ins=[("xin", [D, T], F32), ("wq", [128, 24 * 1024], F32)],
                   outs=[("qfm", [D, T], BF16), ("kfm", [D, T], BF16),
                         ("ktm", [T, D], BF16), ("vtm", [T, D], BF16)]),
    "attn1b": dict(ins=[("qfm", [D, T], BF16), ("kfm", [D, T], BF16),
                        ("ktm", [T, D], BF16), ("vtm", [T, D], BF16),
                        ("recip64", [64, T], F32)],
                   outs=[("ctx", [D, T], BF16)]),
    "attn1c": dict(ins=[("xin", [D, T], F32), ("ctx", [D, T], BF16),
                        ("wo", [128, 8 * 1024], F32)],
                   outs=[("xout", [D, T], F32)]),
}


def emit_phase(em, B, name, d):
    if name == "moe0":
        emit_moe_phase(em, B, d, 9, 22, "m0")
    elif name == "moe1":
        emit_moe_phase(em, B, d, 8, 8, "m1")
    elif name == "attn0a":
        emit_qkv_phase(em, B, d, 1.0 / 32.0, False)
    elif name == "attn0b":
        emit_attn0_phase(em, B, d)
    elif name == "attn1a":
        emit_qkv_phase(em, B, d, 1.0 / 8.0, True)
    elif name == "attn1b":
        emit_attn1_phase(em, B, d)
    elif name == "attn1c":
        emit_proj_phase(em, B, d)
    else:
        raise ValueError(name)


def build_phase_program(name, serial=False):
    nc = bass.Bass("TRN2", target_bir_lowering=False)
    io = PHASE_IO[name]
    d = {}
    for nm, shp, dt in CONST_SPECS:
        d[nm] = nc.dram_tensor(nm, shp, dt, kind="ExternalInput").ap()
    for nm, shp, dt in io["ins"]:
        d[nm] = nc.dram_tensor(nm, shp, dt, kind="ExternalInput").ap()
    for nm, shp, dt in io["outs"]:
        d[nm] = nc.dram_tensor(nm, shp, dt, kind="ExternalOutput").ap()
    for nm, shp, dt in io.get("internal", []):
        d[nm] = nc.dram_tensor(nm, shp, dt, kind="Internal").ap()
    with contextlib.ExitStack() as stack:
        B = alloc_bufs(nc, stack)
        em = Em(nc, serial=serial)
        em.sems(stack, "ld", "st", "pe", "dv", "ac", "gp")
        block = stack.enter_context(nc.Block())
        load_consts(em, B, d, recip=(name == "attn1b"))
        emit_phase(em, B, name, d)
        em.op("gpsimd", None, (("st", em.cnt["st"]),))
        em.emit(block)
    return nc


def build_full_program(serial=False):
    nc = bass.Bass("TRN2", target_bir_lowering=False)
    d = {}
    for nm, shp, dt in CONST_SPECS:
        d[nm] = nc.dram_tensor(nm, shp, dt, kind="ExternalInput").ap()
    d["recip64"] = nc.dram_tensor("recip64", [64, T], F32,
                                  kind="ExternalInput").ap()
    for nm, shp, dt in WEIGHT_SPECS:
        d[nm] = nc.dram_tensor(nm, shp, dt, kind="ExternalInput").ap()
    XIN = nc.dram_tensor("xin", [D, T], F32, kind="ExternalInput").ap()
    XOUT = nc.dram_tensor("xout", [D, T], F32, kind="ExternalOutput").ap()
    XA = nc.dram_tensor("xa", [D, T], F32, kind="Internal").ap()
    XB = nc.dram_tensor("xb", [D, T], F32, kind="Internal").ap()
    qfm = nc.dram_tensor("qfm", [D, T], BF16, kind="Internal").ap()
    kfm = nc.dram_tensor("kfm", [D, T], BF16, kind="Internal").ap()
    ktm = nc.dram_tensor("ktm", [T, D], BF16, kind="Internal").ap()
    vtm = nc.dram_tensor("vtm", [T, D], BF16, kind="Internal").ap()
    ctx = nc.dram_tensor("ctx", [D, T], BF16, kind="Internal").ap()
    for nm, shp in (("m0wgb", [128, 9 * 22 * 1024]), ("m0wub", [128, 9 * 22 * 1024]),
                    ("m0wdb", [128, 9 * 8 * 22 * 128]),
                    ("m1wgb", [128, 8 * 8 * 1024]), ("m1wub", [128, 8 * 8 * 1024]),
                    ("m1wdb", [128, 8 * 8 * 8 * 128])):
        d[nm] = nc.dram_tensor(nm, shp, BF16, kind="Internal").ap()
    m0 = {k: d[k] for k in ("m0wr", "m0wg", "m0wu", "m0wd",
                            "m0wgb", "m0wub", "m0wdb")}
    m1 = {k: d[k] for k in ("m1wr", "m1wg", "m1wu", "m1wd",
                            "m1wgb", "m1wub", "m1wdb")}
    att = dict(qfm=qfm, kfm=kfm, vtm=vtm)
    att1 = dict(qfm=qfm, kfm=kfm, ktm=ktm, vtm=vtm)
    with contextlib.ExitStack() as stack:
        B = alloc_bufs(nc, stack)
        em = Em(nc, serial=serial)
        em.sems(stack, "ld", "st", "pe", "dv", "ac", "gp")
        block = stack.enter_context(nc.Block())
        load_consts(em, B, d, recip=False)

        emit_moe_phase(em, B, dict(xin=XIN, xout=XA, **m0), 9, 22, "m0")
        em.new_phase(stack)
        emit_qkv_phase(em, B, dict(xin=XA, wq=d["wq0"], **att), 1.0 / 32.0, False)
        em.new_phase(stack)
        emit_attn0_phase(em, B, dict(xin=XA, xout=XB, **att))
        em.new_phase(stack)
        emit_moe_phase(em, B, dict(xin=XB, xout=XA, **m0), 9, 22, "m0", first=False)
        em.new_phase(stack)
        emit_moe_phase(em, B, dict(xin=XA, xout=XB, **m1), 8, 8, "m1")
        em.new_phase(stack)
        emit_qkv_phase(em, B, dict(xin=XB, wq=d["wq1"], **att1), 1.0 / 8.0, True)
        em.new_phase(stack)
        em.dma("sync", B.recip[:], d["recip64"][:])
        em.barrier()
        emit_attn1_phase(em, B, dict(ctx=ctx, **att1))
        em.new_phase(stack)
        emit_proj_phase(em, B, dict(xin=XB, xout=XA, ctx=ctx, wo=d["wo1"]))
        em.new_phase(stack)
        emit_moe_phase(em, B, dict(xin=XA, xout=XOUT, **m1), 8, 8, "m1", first=False)
        em.op("gpsimd", None, (("st", em.cnt["st"]),))
        em.emit(block)
    return nc


def swz_nmajor(W):
    K, N = W.shape
    kc, nb = K // 128, N // 128
    A = W.reshape(kc, 128, nb, 128).transpose(1, 2, 0, 3)
    return np.ascontiguousarray(A.reshape(128, nb * kc * 128))


def swz_router(Wr, nE):
    A = Wr.reshape(8, 128, nE).transpose(1, 0, 2)
    out = np.zeros((128, 80), dtype=np.float32)
    out.reshape(128, 8, 10)[:, :, :nE] = A
    return out


def make_consts():
    c = {}
    c["ones_f"] = np.ones((128, 128), np.float32)
    c["ones_b"] = np.ones((128, 128), BFNP)
    c["ident"] = np.eye(128, dtype=np.float32)
    s9 = np.zeros((9, 9 * 128), BFNP)
    for e in range(9):
        s9[e, e * 128:(e + 1) * 128] = 1
    c["sel9"] = s9
    s8 = np.zeros((8, 8 * 128), BFNP)
    for e in range(8):
        s8[e, e * 128:(e + 1) * 128] = 1
    c["sel8"] = s8
    tri = np.zeros((64, 64), np.float32)
    for j in range(64):
        tri[j, j:] = 1.0
    c["triu"] = tri
    rc = (1.0 / np.arange(1, T + 1, dtype=np.float64)).astype(np.float32)
    c["recip64"] = np.ascontiguousarray(np.broadcast_to(rc[None, :], (64, T)))
    return c


def prep_weights(inp):
    w = {}
    w["m0wr"] = swz_router(np.asarray(inp["L0_ffn_Wr"], np.float32), 9)
    Wg = np.asarray(inp["L0_ffn_Wg"], np.float32)
    Wu = np.asarray(inp["L0_ffn_Wu"], np.float32)
    Wd = np.asarray(inp["L0_ffn_Wd"], np.float32)
    w["m0wg"] = np.concatenate([swz_nmajor(Wg[e]) for e in range(9)], axis=1)
    w["m0wu"] = np.concatenate([swz_nmajor(Wu[e]) for e in range(9)], axis=1)
    w["m0wd"] = np.concatenate([swz_nmajor(Wd[e]) for e in range(9)], axis=1)
    w["wq0"] = swz_nmajor(np.asarray(inp["L0_attn_w_qkv"], np.float32))
    w["m1wr"] = swz_router(np.ascontiguousarray(
        np.asarray(inp["L1_ffn_router_w"], np.float32).T), 8)
    up = np.asarray(inp["L1_ffn_up_proj"], np.float32)
    dn = np.asarray(inp["L1_ffn_down_proj"], np.float32)
    w["m1wg"] = np.concatenate([swz_nmajor(up[e][:, :D]) for e in range(8)], axis=1)
    w["m1wu"] = np.concatenate([swz_nmajor(up[e][:, D:]) for e in range(8)], axis=1)
    w["m1wd"] = np.concatenate([swz_nmajor(dn[e]) for e in range(8)], axis=1)
    w["wq1"] = swz_nmajor(np.ascontiguousarray(
        np.asarray(inp["L1_attn_W_qkv"], np.float32).T))
    w["wo1"] = swz_nmajor(np.ascontiguousarray(
        np.asarray(inp["L1_attn_W_o"], np.float32).T))
    return w


def kernel(**inputs):
    consts = make_consts()
    w = prep_weights(inputs)
    x = np.asarray(inputs["L0_attn_embeddings"], np.float32)
    if "full" not in _PROG_CACHE:
        _PROG_CACHE["full"] = build_full_program()
    nc = _PROG_CACHE["full"]
    base = dict(consts)
    base.update(w)
    in_maps = []
    for i in range(NCORES):
        m = dict(base)
        m["xin"] = np.ascontiguousarray(x[i].T)
        in_maps.append(m)
    res = run_bass_kernel_spmd(nc, in_maps, core_ids=list(range(NCORES)))
    out = np.stack([np.asarray(res.results[i]["xout"]).T
                    for i in range(NCORES)], axis=0)
    return np.ascontiguousarray(out.astype(np.float32))


# version 04
# speedup vs baseline: 1.1358x; 1.1358x over previous
import numpy as np
import ml_dtypes
import contextlib
import concourse.bass as bass
import concourse.mybir as mybir
from concourse.bass_utils import run_bass_kernel_spmd

F32 = mybir.dt.float32
BF16 = mybir.dt.bfloat16
AF = mybir.ActivationFunctionType
ALU = mybir.AluOpType
AX = mybir.AxisListType
BFNP = ml_dtypes.bfloat16

T = 4096
D = 1024
NFC = 8
TT = 1024
NTT = T // TT
EPS = 1e-5
NCORES = 8

ENGINES = ("sync", "tensor", "vector", "scalar", "gpsimd")
_PROG_CACHE = {}


class Em:
    def __init__(self, nc, serial=False):
        self.nc = nc
        self.serial = serial
        self.streams = {e: [] for e in ENGINES}
        self.cnt = {}
        self.handles = {}
        self.last_wait = {e: {} for e in ENGINES}
        self.phase = 0

    def sems(self, stack, *names):
        self.phase += 1
        for n in names:
            self.handles[n] = stack.enter_context(
                self.nc.semaphore(f"{n}_p{self.phase}"))
            self.cnt[n] = 0
        self.last_wait = {e: {} for e in ENGINES}

    def new_phase(self, stack):
        self.barrier()
        self.sems(stack, "ld", "st", "pe", "dv", "ac", "gp")

    def op(self, eng, fn, waits=(), inc=None, amt=1):
        if self.serial:
            mine = {"sync": ("ld",), "tensor": ("pe",), "vector": ("dv",),
                    "scalar": ("ac",), "gpsimd": ("gp", "st")}[eng]
            waits = tuple((s, self.cnt[s]) for s in self.cnt if s not in mine)
        w = []
        for s, v in waits:
            if v and v > 0 and self.last_wait[eng].get(s, 0) < v:
                w.append((self.handles[s], v))
                self.last_wait[eng][s] = v
        if inc is not None:
            self.cnt[inc] += amt
        self.streams[eng].append(
            (w, fn, self.handles[inc] if inc is not None else None, amt))
        return self.cnt[inc] if inc is not None else None

    def dma(self, eng, out, in_, waits=(), inc="ld"):
        return self.op(eng, lambda e, o=out, i=in_: e.dma_start(out=o, in_=i),
                       waits, inc, 16)

    def barrier(self):
        snap = dict(self.cnt)
        mine = {"sync": ("ld",), "tensor": ("pe",), "vector": ("dv",),
                "scalar": ("ac",), "gpsimd": ("gp", "st")}
        for eng in ENGINES:
            waits = tuple((s, v) for s, v in snap.items() if s not in mine[eng])
            self.op(eng, None, waits)

    def emit(self, block):
        for eng in ENGINES:
            stream = self.streams[eng]
            if not stream:
                continue
            dec = getattr(block, eng)

            def make(eng_stream):
                def emit_fn(e):
                    for w, fn, inc_h, amt in eng_stream:
                        for h, v in w:
                            e.wait_ge(h, v)
                        inst = fn(e) if fn is not None else None
                        if inc_h is not None:
                            if inst is None:
                                inst = e.nop()
                            inst.then_inc(inc_h, amt)
                return emit_fn
            dec(make(stream))


def mm_op(em, out, lhsT, rhs, start, stop, waits=()):
    return em.op("tensor",
                 lambda e, o=out, l=lhsT, r=rhs, s1=start, s2=stop:
                 e.matmul(out=o, lhsT=l, rhs=r, start=s1, stop=s2),
                 waits, "pe")


def tr_op(em, out, in_, ident, waits=()):
    return em.op("tensor",
                 lambda e, o=out, i=in_, d=ident:
                 e.transpose(out=o, in_=i, identity=d),
                 waits, "pe")


def tt_op(em, out, in0, in1, aluop, waits=()):
    return em.op("vector",
                 lambda e, o=out, a=in0, b=in1, p=aluop:
                 e.tensor_tensor(out=o, in0=a, in1=b, op=p),
                 waits, "dv")


def ts_op(em, out, in0, s1, aluop, waits=()):
    return em.op("vector",
                 lambda e, o=out, a=in0, x=s1, p=aluop:
                 e.tensor_scalar(out=o, in0=a, scalar1=x, scalar2=None, op0=p),
                 waits, "dv")


def red_op(em, out, in_, waits=(), op="max"):
    if op == "max":
        return em.op("vector",
                     lambda e, o=out, i=in_: e.reduce_max(out=o, in_=i, axis=AX.X),
                     waits, "dv")
    return em.op("vector",
                 lambda e, o=out, i=in_: e.reduce_sum(out=o, in_=i, axis=AX.X),
                 waits, "dv")


def cp_op(em, out, in_, waits=(), eng="vector", sem="dv"):
    return em.op(eng, lambda e, o=out, i=in_: e.tensor_copy(out=o, in_=i),
                 waits, sem)


def rc_op(em, out, in_, waits=()):
    return em.op("vector", lambda e, o=out, i=in_: e.reciprocal(out=o, in_=i),
                 waits, "dv")


def rca_op(em, out, in_, scratch, waits=()):
    return em.op("vector",
                 lambda e, o=out, i=in_, s=scratch:
                 e.reciprocal_approx_accurate(out=o, in_=i, scratch=s),
                 waits, "dv")


def ms_op(em, ap, val, waits=(), eng="vector", sem="dv"):
    return em.op(eng, lambda e, a=ap, v=val: e.memset(a, v), waits, sem)


def act_op(em, out, in_, func, waits=(), bias=None, scale=None, accum=None):
    kw = {}
    if bias is not None:
        kw["bias"] = bias
    if scale is not None:
        kw["scale"] = scale
    if accum is not None:
        kw["accum_out"] = accum
    return em.op("scalar",
                 lambda e, o=out, i=in_, f=func, k=dict(kw):
                 e.activation(out=o, in_=i, func=f, **k),
                 waits, "ac")


class Bufs:
    pass


def alloc_bufs(nc, stack):
    E_ = stack.enter_context
    B = Bufs()
    sb = nc.sbuf_tensor
    B.xtt = E_(sb("xtt", [128, 8 * TT], F32))
    B.h = E_(sb("h", [128, 8 * TT], BF16))
    B.he = E_(sb("he", [128, 8 * TT], BF16))
    B.big = E_(sb("big", [128, 24 * TT], BF16))
    B.sq = E_(sb("sq", [128, TT], F32))
    B.sg = E_(sb("sg", [128, 2 * TT], F32))
    B.wstage = E_(sb("wstage", [128, 2048], F32))
    B.wgu = E_(sb("wgu", [128, 4096], BF16))
    B.wdstage = E_(sb("wdstage", [128, 2816], F32))
    B.wd = E_(sb("wd", [128, 5632], BF16))
    B.cwb = E_(sb("cwb", [128, TT], BF16))
    B.cw9b = E_(sb("cw9b", [9, T], BF16))
    B.wrf = E_(sb("wrf", [128, 80], F32))
    B.sbc = E_(sb("sbc", [128, 16], F32))
    B.srow = E_(sb("srow", [1, 16], F32))
    B.rowsA = E_(sb("rowsA", [1, 2 * TT], F32))
    B.lg = E_(sb("lg", [128, 128], F32))
    B.m2s = E_(sb("m2s", [128, 32], F32))
    B.cwtc = E_(sb("cwtc", [128, 32], F32))
    B.zr = E_(sb("zr", [1, 512], F32))
    B.zs = E_(sb("zs", [1, 1024], F32))
    B.zrb = E_(sb("zrb", [1, 512], BF16))
    B.zb = E_(sb("zb", [64, 512], F32))
    B.xsl = E_(sb("xsl", [64, 512], F32))
    B.tmp64 = E_(sb("tmp64", [64, 512], F32))
    B.cmx = E_(sb("cmx", [128, 64], F32))
    B.ctb = E_(sb("ctb", [32, 128], BF16))
    B.smT = E_(sb("smT", [64, 128], BF16))
    B.pbf = E_(sb("pbf", [64, 64], BF16))
    B.ones_f = E_(sb("s_ones_f", [128, 128], F32))
    B.ones_b = E_(sb("s_ones_b", [128, 128], BF16))
    B.ident = E_(sb("s_ident", [128, 128], F32))
    B.sel9 = E_(sb("s_sel9", [9, 9 * 128], BF16))
    B.sel8 = E_(sb("s_sel8", [8, 8 * 128], BF16))
    B.triu = E_(sb("s_triu", [64, 64], F32))
    B.qb = B.big[0:65, 0:T]
    B.kb = B.big[0:65, T:2 * T]
    B.vt = B.big[:, 2 * T:2 * T + 32 * 65]
    B.probs = B.big[:, 2 * T + 2080:2 * T + 2080 + 3 * TT]
    B.recip = B.xtt[0:64, 0:T]
    B.psA = E_(nc.psum_tensor("psA", [128, 2048], F32))
    B.psB = E_(nc.psum_tensor("psB", [128, 2048], F32))
    return B


def load_consts(em, B, d, recip=False):
    for name, dst in (("ones_f", B.ones_f), ("ones_b", B.ones_b),
                      ("ident", B.ident), ("sel9", B.sel9), ("sel8", B.sel8),
                      ("triu", B.triu)):
        em.dma("sync", dst[:], d[name][:])
    if recip:
        em.dma("sync", B.recip[:], d["recip64"][:])
    return em.cnt["ld"]


def emit_ln(em, B, x_in, tt, xtt_free=(), ps_wait=()):
    x3 = B.xtt.rearrange("p (c t) -> p c t", c=8)
    src = x_in.rearrange("(c p) t -> p c t", p=128)[:, :, tt * TT:(tt + 1) * TT]
    vx = em.dma("sync", x3[:], src, waits=tuple(xtt_free))

    for fc in range(NFC):
        xc = B.xtt[:, fc * TT:(fc + 1) * TT]
        for hh in range(2):
            mm_op(em, B.psA[0:1, hh * 512:(hh + 1) * 512],
                  B.ones_f[:, 0:1], xc[:, hh * 512:(hh + 1) * 512],
                  start=(fc == 0), stop=(fc == NFC - 1),
                  waits=(("ld", vx),) + tuple(ps_wait))
    vsq_mm = None
    for fc in range(NFC):
        xc = B.xtt[:, fc * TT:(fc + 1) * TT]
        w = [("ld", vx)]
        if vsq_mm is not None:
            w.append(("pe", vsq_mm))
        vdv = tt_op(em, B.sq[:], xc, xc, ALU.mult, waits=tuple(w))
        for hh in range(2):
            vsq_mm = mm_op(em, B.psA[0:1, 1024 + hh * 512:1024 + (hh + 1) * 512],
                           B.ones_f[:, 0:1], B.sq[:, hh * 512:(hh + 1) * 512],
                           start=(fc == 0), stop=(fc == NFC - 1),
                           waits=(("dv", vdv),))
    mean = B.rowsA[:, 0:TT]
    rstd = B.rowsA[:, TT:2 * TT]
    aux = B.sq[0:1, 0:TT]
    u0 = ts_op(em, mean, B.psA[0:1, 0:1024], 1.0 / D, ALU.mult,
               waits=(("pe", vsq_mm),))
    u1 = ts_op(em, aux, B.psA[0:1, 1024:2048], 1.0 / D, ALU.mult)
    u2 = tt_op(em, rstd, mean, mean, ALU.mult, waits=(("dv", u0),))
    u3 = tt_op(em, aux, aux, rstd, ALU.subtract, waits=(("dv", u2),))
    u4 = ts_op(em, aux, aux, EPS, ALU.add, waits=(("dv", u3),))
    v1 = rc_op(em, aux, aux, waits=(("dv", u4),))
    va = act_op(em, rstd, aux, AF.Sqrt, waits=(("dv", v1),))
    vbc = None
    for hh in range(2):
        mm_op(em, B.psB[:, hh * 512:(hh + 1) * 512],
              B.ones_f[0:1, :], mean[:, hh * 512:(hh + 1) * 512],
              start=True, stop=True, waits=(("dv", v1),))
        vbc = mm_op(em, B.psB[:, 1024 + hh * 512:1024 + (hh + 1) * 512],
                    B.ones_f[0:1, :], rstd[:, hh * 512:(hh + 1) * 512],
                    start=True, stop=True, waits=(("ac", va),))
    vh = None
    for fc in range(NFC):
        xc = B.xtt[:, fc * TT:(fc + 1) * TT]
        hc = B.h[:, fc * TT:(fc + 1) * TT]
        tt_op(em, B.sq[:], xc, B.psB[:, 0:1024], ALU.subtract,
              waits=(("pe", vbc),))
        vh = tt_op(em, hc, B.sq[:], B.psB[:, 1024:2048], ALU.mult)
    return vh, vx


def emit_router(em, B, nE, tt, vh, wr_ready, sbc_ready, vx):
    v_free = None
    v_trc = None
    vsig_in = None
    for tc in range(8):
        lg_ps = B.psA[:, 0:nE]
        for fc in range(NFC):
            w = [("dv", vh), ("ld", max(wr_ready, vx)), ("dv", sbc_ready)]
            if v_free is not None:
                w.append(("dv", v_free))
            vmm = mm_op(em, lg_ps,
                        B.xtt[:, fc * TT + tc * 128: fc * TT + tc * 128 + 128],
                        B.wrf[:, fc * 10: fc * 10 + nE],
                        start=(fc == 0), stop=(fc == NFC - 1),
                        waits=tuple(w) if fc == 0 else ())
        w = [("dv", vh)]
        if v_trc is not None:
            w.append(("dv", v_trc))
        vt1 = tr_op(em, B.psA[0:128, 512:513],
                    B.rowsA[0:1, tc * 128:(tc + 1) * 128], B.ident[0:1, 0:1],
                    waits=tuple(w))
        vt2 = tr_op(em, B.psA[0:128, 513:514],
                    B.rowsA[0:1, TT + tc * 128: TT + (tc + 1) * 128],
                    B.ident[0:1, 0:1])
        mcol = B.lg[:, 112:113]
        rcol = B.lg[:, 113:114]
        v0 = cp_op(em, mcol, B.psA[0:128, 512:513], waits=(("pe", vt2),))
        v_trc = cp_op(em, rcol, B.psA[0:128, 513:514])
        u1 = B.lg[:, 16:16 + nE]
        lgf = B.lg[:, 0:nE]
        v1 = ts_op(em, u1, B.sbc[:, 0:nE], mcol, ALU.mult, waits=(("dv", v0),))
        v_free = tt_op(em, lgf, lg_ps, u1, ALU.subtract,
                       waits=(("pe", vmm), ("dv", v1)))
        v2 = ts_op(em, lgf, lgf, rcol, ALU.mult, waits=(("dv", v_trc),))
        d = B.lg[:, 32:32 + nE]
        ind1 = B.sq[:, tc * 64: tc * 64 + nE]
        ind2 = B.sq[:, tc * 64 + 32: tc * 64 + 32 + nE]
        msk = B.lg[:, 48:48 + nE]
        m1 = B.lg[:, 64:65]
        v3 = red_op(em, m1, lgf, waits=(("dv", v2),))
        v4 = ts_op(em, d, lgf, m1, ALU.subtract, waits=(("dv", v3),))
        v5 = ts_op(em, ind1, d, 0.0, ALU.is_equal, waits=(("dv", v4),))
        v6 = ts_op(em, msk, ind1, -1e30, ALU.mult, waits=(("dv", v5),))
        v7 = tt_op(em, msk, msk, d, ALU.add, waits=(("dv", v6),))
        m2 = B.m2s[:, tc:tc + 1]
        v8 = red_op(em, m2, msk, waits=(("dv", v7),))
        vsig_in = ts_op(em, ind2, msk, m2, ALU.is_equal, waits=(("dv", v8),))
    w2 = B.m2s[:, 8:16]
    w1 = B.m2s[:, 16:24]
    va = act_op(em, w2, B.m2s[:, 0:8], AF.Sigmoid, waits=(("dv", vsig_in),))
    v9 = ts_op(em, w1, w2, -0.5, ALU.mult, waits=(("ac", va),))
    v10 = ts_op(em, w1, w1, 0.5, ALU.add, waits=(("dv", v9),))
    vw = ts_op(em, w2, w2, 0.5, ALU.mult, waits=(("dv", v10),))
    vtr = None
    vb = None
    for tc in range(8):
        ind1 = B.sq[:, tc * 64: tc * 64 + nE]
        ind2 = B.sq[:, tc * 64 + 32: tc * 64 + 32 + nE]
        cwa = B.cwtc[:, (tc % 2) * 16: (tc % 2) * 16 + nE]
        scr = B.lg[:, 96:96 + nE]
        w = [("dv", vw)]
        if vtr is not None and tc >= 2:
            w.append(("pe", vtr))
        va1 = ts_op(em, cwa, ind1, B.m2s[:, 16 + tc:17 + tc], ALU.mult, waits=tuple(w))
        va2 = ts_op(em, scr, ind2, B.m2s[:, 8 + tc:9 + tc], ALU.mult, waits=(("dv", va1),))
        vcw = tt_op(em, cwa, cwa, scr, ALU.add, waits=(("dv", va2),))
        vtr = tr_op(em, B.psA[0:nE, 1024:1152], cwa, B.ident[:], waits=(("dv", vcw),))
        vb = cp_op(em, B.cw9b[0:nE, tt * TT + tc * 128: tt * TT + tc * 128 + 128],
                   B.psA[0:nE, 1024:1152], waits=(("pe", vtr),))
    return vb


def emit_experts(em, B, tt, nE, n_ic, wg_d, wu_d, wd_d, vh, vcw, vx_prev,
                 scr=None, mode="cast_store"):
    K_IC = 1024
    stage_slot = [B.wstage[:, 0:1024], B.wstage[:, 1024:2048]]
    wgu_slot = [B.wgu[:, i * 1024:(i + 1) * 1024] for i in range(4)]
    half = n_ic // 2 * 128
    wd_slot = [B.wd[:, 0:n_ic * 128], B.wd[:, n_ic * 128:2 * n_ic * 128]]
    stage_free = [None, None]
    wgu_free = [None] * 4
    wdst_free = [None, None]
    wd_free = [None, None]
    sg_free = {}
    ps_free = {}
    vx_last = vx_prev
    sel = B.sel9 if nE == 9 else B.sel8

    for e in range(nE):
        w = [("dv", vcw)]
        if vx_last is not None:
            w.append(("dv", vx_last))
        if "cwb" in ps_free:
            w.append(ps_free["cwb"])
        vmm = None
        for hh in range(2):
            vmm = mm_op(em, B.psB[:, hh * 512:(hh + 1) * 512],
                        sel[0:nE, e * 128:(e + 1) * 128],
                        B.cw9b[0:nE, tt * TT + hh * 512: tt * TT + (hh + 1) * 512],
                        start=True, stop=True, waits=tuple(w) if hh == 0 else ())
        vcwb = cp_op(em, B.cwb[:], B.psB[:, 0:1024], waits=(("pe", vmm),))
        ps_free["cwb"] = ("dv", vcwb)
        vhe = None
        for fc in range(NFC):
            vhe = tt_op(em, B.he[:, fc * TT:(fc + 1) * TT],
                        B.h[:, fc * TT:(fc + 1) * TT], B.cwb[:], ALU.mult,
                        waits=(("dv", vh), ("dv", vcwb)))
        vhid_last = None
        for ic in range(n_ic):
            for mat, wsrc, rhs_buf in ((0, wg_d, B.h), (1, wu_d, B.he)):
                s = mat
                b = mat * 2 + (ic % 2)
                off = (e * n_ic + ic) * K_IC
                wfree = list(wgu_free[b]) if wgu_free[b] is not None else []
                if mode == "bf16":
                    vld = em.dma("sync", wgu_slot[b][:],
                                 scr[mat][:, off: off + K_IC], waits=tuple(wfree))
                    ready = ("ld", vld)
                    vcast = None
                else:
                    w = []
                    if stage_free[s] is not None:
                        w.append(stage_free[s])
                    vld = em.dma("sync", stage_slot[s][:],
                                 wsrc[:, off: off + K_IC], waits=tuple(w))
                    w2 = [("ld", vld)] + wfree
                    vcast = cp_op(em, wgu_slot[b][:], stage_slot[s][:],
                                  waits=tuple(w2), eng="gpsimd", sem="gp")
                    stage_free[s] = ("gp", vcast)
                    ready = ("gp", vcast)
                ps = B.psA if (ic % 2 == 0) else B.psB
                base = 0 if mat == 0 else 1024
                for hh in range(2):
                    key = ("g" if mat == 0 else "u", ic % 2, hh)
                    wm = [ready, ("dv", vhe if mat == 1 else vh)]
                    if key in ps_free:
                        wm.append(ps_free[key])
                    for fc in range(NFC):
                        vmm = mm_op(em, ps[:, base + hh * 512:base + (hh + 1) * 512],
                                    wgu_slot[b][:, fc * 128:(fc + 1) * 128],
                                    rhs_buf[:, fc * TT + hh * 512:
                                            fc * TT + (hh + 1) * 512],
                                    start=(fc == 0), stop=(fc == NFC - 1),
                                    waits=tuple(wm) if fc == 0 else ())
                    if mat == 0:
                        skey = (ic % 2, hh)
                        wsl = [("pe", vmm)]
                        if skey in sg_free:
                            wsl.append(sg_free[skey])
                        vsl = act_op(em, B.sg[:, (ic % 2) * TT + hh * 512:
                                              (ic % 2) * TT + (hh + 1) * 512],
                                     ps[:, hh * 512:(hh + 1) * 512], AF.Silu,
                                     waits=tuple(wsl))
                        ps_free[key] = ("ac", vsl)
                    else:
                        vsl = ps_free[("g", ic % 2, hh)][1]
                        vhd = tt_op(em, B.big[:, ic * TT + hh * 512:
                                           ic * TT + (hh + 1) * 512],
                                    B.sg[:, (ic % 2) * TT + hh * 512:
                                         (ic % 2) * TT + (hh + 1) * 512],
                                    ps[:, 1024 + hh * 512:1024 + (hh + 1) * 512],
                                    ALU.mult,
                                    waits=(("pe", vmm), ("ac", vsl)))
                        ps_free[key] = ("dv", vhd)
                        ps_free[("g", ic % 2, hh)] = ("dv", vhd)
                        sg_free[(ic % 2, hh)] = ("dv", vhd)
                        vhid_last = vhd
                wgu_free[b] = [("pe", vmm)]
                if mode == "cast_store":
                    vst = em.op("gpsimd",
                                lambda e, o=scr[mat][:, off: off + K_IC],
                                i=wgu_slot[b][:]: e.dma_start(out=o, in_=i),
                                (("gp", vcast),), "st", 16)
                    wgu_free[b].append(("st", vst))
        for oc in range(8):
            s = oc % 2
            off = (e * 8 + oc) * n_ic * 128
            wdfree = list(wd_free[s]) if wd_free[s] is not None else []
            if mode == "bf16":
                vld = None
                for hf in range(2):
                    vld = em.dma("sync", wd_slot[s][:, hf * half:(hf + 1) * half],
                                 scr[2][:, off + hf * half: off + (hf + 1) * half],
                                 waits=tuple(wdfree) if hf == 0 else ())
                ready = ("ld", vld)
                vcast = None
            else:
                vcast = None
                for hf in range(2):
                    w = []
                    if wdst_free[hf] is not None:
                        w.append(wdst_free[hf])
                    vld = em.dma("sync", B.wdstage[:, hf * half:(hf + 1) * half],
                                 wd_d[:, off + hf * half: off + (hf + 1) * half],
                                 waits=tuple(w))
                    w2 = [("ld", vld)] + (wdfree if hf == 0 else [])
                    vcast = cp_op(em, wd_slot[s][:, hf * half:(hf + 1) * half],
                                  B.wdstage[:, hf * half:(hf + 1) * half],
                                  waits=tuple(w2), eng="gpsimd", sem="gp")
                    wdst_free[hf] = ("gp", vcast)
                ready = ("gp", vcast)
            ps = B.psA if (oc % 2 == 0) else B.psB
            for hh in range(2):
                key = ("d", oc % 2, hh)
                wm = [ready, ("dv", vhid_last)]
                if key in ps_free:
                    wm.append(ps_free[key])
                for ic in range(n_ic):
                    vmm = mm_op(em, ps[:, hh * 512:(hh + 1) * 512],
                                wd_slot[s][:, ic * 128:(ic + 1) * 128],
                                B.big[:, ic * TT + hh * 512: ic * TT + (hh + 1) * 512],
                                start=(ic == 0), stop=(ic == n_ic - 1),
                                waits=tuple(wm) if ic == 0 else ())
                xs = B.xtt[:, oc * TT + hh * 512: oc * TT + (hh + 1) * 512]
                vx_last = tt_op(em, xs, xs, ps[:, hh * 512:(hh + 1) * 512], ALU.add,
                                waits=(("pe", vmm),))
                ps_free[key] = ("dv", vx_last)
            wd_free[s] = [("pe", vmm)]
            if mode == "cast_store":
                vst = em.op("gpsimd",
                            lambda e, o=scr[2][:, off: off + n_ic * 128],
                            i=wd_slot[s][:, 0:n_ic * 128]: e.dma_start(out=o, in_=i),
                            (("gp", vcast),), "st", 16)
                wd_free[s].append(("st", vst))
    return vx_last


def emit_moe_phase(em, B, d, nE, n_ic, pre, first=True):
    x_in, x_out = d["xin"], d["xout"]
    wr_d, wg_d, wu_d, wd_d = d[pre + "wr"], d[pre + "wg"], d[pre + "wu"], d[pre + "wd"]
    scr = (d[pre + "wgb"], d[pre + "wub"], d[pre + "wdb"])
    wr_ready = em.dma("sync", B.wrf[:], wr_d[:])
    vs = None
    for fc in range(NFC):
        vs = mm_op(em, B.psA[0:1, 0:nE], B.ones_f[:, 0:1],
                   B.wrf[:, fc * 10: fc * 10 + nE],
                   start=(fc == 0), stop=(fc == NFC - 1),
                   waits=(("ld", wr_ready),))
    vsr = cp_op(em, B.srow[0:1, 0:nE], B.psA[0:1, 0:nE], waits=(("pe", vs),))
    vsb = mm_op(em, B.psA[:, 0:nE], B.ones_f[0:1, :], B.srow[0:1, 0:nE],
                start=True, stop=True, waits=(("dv", vsr),))
    sbc_ready = cp_op(em, B.sbc[:, 0:nE], B.psA[:, 0:nE], waits=(("pe", vsb),))
    vstore = None
    for tt in range(NTT):
        xfree = [("st", vstore)] if vstore is not None else []
        psw = [("dv", sbc_ready)] if tt == 0 else []
        vh, vx = emit_ln(em, B, x_in, tt, xtt_free=xfree, ps_wait=psw)
        vcw = emit_router(em, B, nE, tt, vh, wr_ready, sbc_ready, vx)
        mode = "cast_store" if (first and tt == 0) else "bf16"
        vx_last = emit_experts(em, B, tt, nE, n_ic, wg_d, wu_d, wd_d, vh, vcw,
                               None, scr=scr, mode=mode)
        x3 = B.xtt.rearrange("p (c t) -> p c t", c=8)
        dst = x_out.rearrange("(c p) t -> p c t", p=128)[:, :, tt * TT:(tt + 1) * TT]
        vstore = em.op("gpsimd", lambda e, o=dst, i=x3[:]: e.dma_start(out=o, in_=i),
                       (("dv", vx_last),), "st", 16)
    em.barrier()


def emit_qkv_phase(em, B, d, qscale, sign_kv):
    x_in, wq_d = d["xin"], d["wq"]
    q_fm, k_fm, v_tm = d["qfm"], d["kfm"], d["vtm"]
    k_tm = d.get("ktm")
    vcasts = []
    stage_free = [None, None]
    for sl in range(24):
        s = sl % 2
        w = []
        if stage_free[s] is not None:
            w.append(stage_free[s])
        vld = em.dma("sync", B.wstage[:, s * 1024:(s + 1) * 1024],
                     wq_d[:, sl * 1024:(sl + 1) * 1024], waits=tuple(w))
        vc = cp_op(em, B.big[:, sl * 1024:(sl + 1) * 1024],
                   B.wstage[:, s * 1024:(s + 1) * 1024],
                   waits=(("ld", vld),), eng="gpsimd", sem="gp")
        vcasts.append(vc)
        stage_free[s] = ("gp", vc)
    he2 = B.he
    slot_free = {}
    ps_free = {}
    vh = None
    va = None
    for tt in range(NTT):
        xfree = (("dv", vh),) if vh is not None else ()
        pswait = (("ac", va),) if va is not None else ()
        vh, vx = emit_ln(em, B, x_in, tt, xtt_free=xfree, ps_wait=pswait)
        for jc in range(16):
            ps = B.psA if jc % 2 == 0 else B.psB
            key = ("qk", jc % 2)
            for hh in range(2):
                wm = [("dv", vh), ("gp", vcasts[jc])]
                if key in ps_free:
                    wm.append(ps_free[key])
                for fc in range(NFC):
                    vmm = mm_op(em, ps[:, hh * 512:(hh + 1) * 512],
                                B.big[:, jc * 1024 + fc * 128: jc * 1024 + fc * 128 + 128],
                                B.h[:, fc * TT + hh * 512: fc * TT + (hh + 1) * 512],
                                start=(fc == 0), stop=(fc == NFC - 1),
                                waits=tuple(wm) if fc == 0 else ())
            sl = jc % 4
            wv = [("pe", vmm)]
            if sl in slot_free:
                wv.append(slot_free[sl])
            dst_sb = he2[:, sl * 1024:(sl + 1) * 1024]
            if jc < 8:
                va = act_op(em, dst_sb, ps[:, 0:1024], AF.Copy,
                            scale=qscale, waits=tuple(wv))
            elif sign_kv:
                va = act_op(em, dst_sb, ps[:, 0:1024], AF.Sign, waits=tuple(wv))
            else:
                va = act_op(em, dst_sb, ps[:, 0:1024], AF.Copy, waits=tuple(wv))
            ps_free[key] = ("ac", va)
            dram = q_fm if jc < 8 else k_fm
            r0 = (jc % 8) * 128
            vst = em.op("gpsimd",
                        lambda e, o=dram[r0:r0 + 128, tt * TT:(tt + 1) * TT], i=dst_sb:
                        e.dma_start(out=o, in_=i),
                        (("ac", va),), "st", 16)
            slot_free[sl] = ("st", vst)
        tok_outs = [("v", v_tm, 16)]
        if k_tm is not None:
            tok_outs = [("k", k_tm, 8), ("v", v_tm, 16)]
        for name, dram, jbase in tok_outs:
            for tc in range(8):
                ps = B.psA if tc % 2 == 0 else B.psB
                key = ("tok" + name, tc % 2)
                wm = [("dv", vh), ("gp", vcasts[jbase + 7])]
                if key in ps_free:
                    wm.append(ps_free[key])
                for jc in range(8):
                    for fc in range(NFC):
                        vmm = mm_op(em, ps[:, 1024 + jc * 128:1024 + (jc + 1) * 128],
                                    B.h[:, fc * TT + tc * 128: fc * TT + tc * 128 + 128],
                                    B.big[:, (jbase + jc) * 1024 + fc * 128:
                                         (jbase + jc) * 1024 + fc * 128 + 128],
                                    start=(fc == 0), stop=(fc == NFC - 1),
                                    waits=tuple(wm) if (fc == 0 and jc == 0) else ())
                sl = 4 + tc % 2 + (2 if name == "v" else 0)
                wv = [("pe", vmm)]
                if sl in slot_free:
                    wv.append(slot_free[sl])
                dst_sb = he2[:, sl * 1024:(sl + 1) * 1024]
                if sign_kv:
                    va = act_op(em, dst_sb, ps[:, 1024:2048], AF.Sign, waits=tuple(wv))
                else:
                    va = act_op(em, dst_sb, ps[:, 1024:2048], AF.Copy, waits=tuple(wv))
                ps_free[key] = ("ac", va)
                r0 = tt * TT + tc * 128
                vst = em.op("gpsimd",
                            lambda e, o=dram[r0:r0 + 128, :], i=dst_sb:
                            e.dma_start(out=o, in_=i),
                            (("ac", va),), "st", 16)
                slot_free[sl] = ("st", vst)
    em.barrier()


def emit_attn0_phase(em, B, d):
    x_in, x_out = d["xin"], d["xout"]
    q_fm, k_fm, v_tm = d["qfm"], d["kfm"], d["vtm"]
    vt3 = B.vt.rearrange("p (c q) -> p c q", q=65)
    vones = ms_op(em, vt3[:, :, 64:65], 1.0)
    qk_free = None
    ps_free = {}
    buf_free = {}
    vstore = None
    for hd in range(16):
        w = [("dv", vones)] if hd == 0 else [("pe", qk_free)]
        vq = em.dma("sync", B.qb[0:64, :], q_fm[hd * 64:(hd + 1) * 64, :],
                    waits=tuple(w))
        vk = em.dma("sync", B.kb[0:64, :], k_fm[hd * 64:(hd + 1) * 64, :])
        vv = em.dma("sync", vt3[:, :, 0:64],
                    v_tm.rearrange("(c p) d -> p c d", p=128)[:, :, hd * 64:(hd + 1) * 64])
        for qt in range(8):
            smm = {}
            exps = {}

            def sreg(par):
                if par < 2:
                    return B.psA[:, par * 1024:(par + 1) * 1024]
                return B.psB[:, 1024:2048]

            def S(pair, vq=vq, vk=vk, qt=qt):
                par = pair % 3
                ps_s = sreg(par)
                w1 = [("ld", vq), ("ld", vk)]
                if ("s", par) in ps_free:
                    w1.append(ps_free[("s", par)])
                v = None
                for kc2 in range(2):
                    kc = pair * 2 + kc2
                    v = mm_op(em, ps_s[:, kc2 * 512:(kc2 + 1) * 512],
                              B.kb[0:64, kc * 128:(kc + 1) * 128],
                              B.qb[0:64, qt * 512:(qt + 1) * 512],
                              start=True, stop=True,
                              waits=tuple(w1) if kc2 == 0 else ())
                smm[pair] = v

            def E(pair):
                par = pair % 3
                ps_s = sreg(par)
                pr = B.probs[:, par * 1024:(par + 1) * 1024]
                w2 = [("pe", smm[pair])]
                if ("p", par) in buf_free:
                    w2.append(buf_free[("p", par)])
                va = act_op(em, pr, ps_s, AF.Exp, waits=tuple(w2))
                ps_free[("s", par)] = ("ac", va)
                exps[pair] = va

            def PV(pair, vv=vv):
                par = pair % 3
                pr = B.probs[:, par * 1024:(par + 1) * 1024]
                v = None
                for kc2 in range(2):
                    kc = pair * 2 + kc2
                    w3 = [("ac", exps[pair]), ("ld", vv)]
                    if pair == 0 and kc2 == 0 and "o" in ps_free:
                        w3.append(ps_free["o"])
                    v = mm_op(em, B.psB[0:65, 0:512],
                              B.vt[:, kc * 65: kc * 65 + 65],
                              pr[:, kc2 * 512:(kc2 + 1) * 512],
                              start=(pair == 0 and kc2 == 0),
                              stop=(pair == 15 and kc2 == 1),
                              waits=tuple(w3) if kc2 == 0 else ())
                buf_free[("p", par)] = ("pe", v)
                return v

            S(0)
            E(0)
            S(1)
            E(1)
            S(2)
            E(2)
            for pair in range(13):
                PV(pair)
                S(pair + 3)
                E(pair + 3)
            PV(13)
            PV(14)
            vpv = PV(15)
            vr = rc_op(em, B.zr[0:1, 0:512], B.psB[64:65, 0:512],
                       waits=(("pe", vpv),))
            vrb = cp_op(em, B.zrb[0:1, 0:512], B.zr[0:1, 0:512],
                        waits=(("dv", vr),))
            w4 = [("dv", vrb)]
            if "zb" in ps_free:
                w4.append(ps_free["zb"])
            vzm = mm_op(em, B.psB[0:64, 512:1024], B.ones_b[0:1, 0:64],
                        B.zrb[0:1, 0:512], start=True, stop=True, waits=tuple(w4))
            vzb = cp_op(em, B.zb[:, 0:512], B.psB[0:64, 512:1024],
                        waits=(("pe", vzm),))
            ps_free["zb"] = ("dv", vzb)
            vtm_ = tt_op(em, B.tmp64[:, 0:512], B.psB[0:64, 0:512], B.zb[:, 0:512],
                         ALU.mult, waits=(("dv", vzb),))
            ps_free["o"] = ("dv", vtm_)
            w5 = []
            if "xsl" in buf_free:
                w5.append(buf_free["xsl"])
            vxl = em.dma("sync", B.xsl[:, 0:512],
                         x_in[hd * 64:(hd + 1) * 64, qt * 512:(qt + 1) * 512],
                         waits=tuple(w5))
            vxa = tt_op(em, B.xsl[:, 0:512], B.xsl[:, 0:512], B.tmp64[:, 0:512],
                        ALU.add, waits=(("ld", vxl), ("dv", vtm_)))
            vstore = em.op("gpsimd",
                           lambda e, o=x_out[hd * 64:(hd + 1) * 64,
                                             qt * 512:(qt + 1) * 512],
                           i=B.xsl[:, 0:512]: e.dma_start(out=o, in_=i),
                           (("dv", vxa),), "st", 16)
            buf_free["xsl"] = ("st", vstore)
        qk_free = vpv
    em.barrier()


def emit_attn1_phase(em, B, d):
    q_fm, k_fm, k_tm, v_tm = d["qfm"], d["kfm"], d["ktm"], d["vtm"]
    ctx_d = d["ctx"]
    ctx_sb = B.h[0:64, 0:4096]

    def kchunk(n):
        return B.he[0:64, n * 64:(n + 1) * 64]

    def vchunk(n):
        return B.he[0:64, 4096 + n * 64: 4096 + (n + 1) * 64]
    buf_free = {}
    ps_free = {}
    last_mm = None
    for hd in range(16):
        w = [] if last_mm is None else [("pe", last_mm)]
        vq = em.dma("sync", B.qb[0:64, :], q_fm[hd * 64:(hd + 1) * 64, :],
                    waits=tuple(w))
        vk = em.dma("sync", B.kb[0:64, :], k_fm[hd * 64:(hd + 1) * 64, :])
        vkt = em.dma("sync",
                     B.he[0:64, 0:4096].rearrange("p (c j) -> p c j", j=64),
                     k_tm.rearrange("(c p) d -> p c d", p=64)[:, :, hd * 64:(hd + 1) * 64],
                     waits=tuple(w))
        vvt = em.dma("sync",
                     B.he[0:64, 4096:8192].rearrange("p (c j) -> p c j", j=64),
                     v_tm.rearrange("(c p) d -> p c d", p=64)[:, :, hd * 64:(hd + 1) * 64])
        wc = [] if last_mm is None else [("pe", last_mm)]
        vpb = ms_op(em, B.pbf[:], 0.0, waits=tuple(wc))
        vctx = None
        for n in range(64):
            par = n % 2
            ps_sT = B.psA[0:64, par * 512: par * 512 + 64]
            w1 = [("ld", vq), ("ld", vk)]
            if ("sT", par) in ps_free:
                w1.append(ps_free[("sT", par)])
            vs = mm_op(em, ps_sT, B.kb[0:64, n * 64:(n + 1) * 64],
                       B.qb[0:64, n * 64:(n + 1) * 64], start=True, stop=True,
                       waits=tuple(w1))
            sm = B.smT[:, par * 64: par * 64 + 64]
            w2 = [("pe", vs)]
            if ("sm", par) in buf_free:
                w2.append(buf_free[("sm", par)])
            vsm = tt_op(em, sm, ps_sT, B.triu[:], ALU.mult, waits=tuple(w2))
            ps_free[("sT", par)] = ("dv", vsm)
            vpe = cp_op(em, B.pbf[:], B.psB[0:64, 0:64],
                        waits=(("pe", last_mm),)) if n > 0 else vpb
            ps_o = B.psA[0:64, 1024 + par * 512: 1024 + par * 512 + 64]
            w3 = [("ld", vkt), ("ld", vvt), ("dv", vsm)]
            if ("o", par) in ps_free:
                w3.append(ps_free[("o", par)])
            mm_op(em, ps_o, vchunk(n), sm, start=True, stop=False,
                  waits=tuple(w3))
            vo = mm_op(em, ps_o, B.pbf[:], B.qb[0:64, n * 64:(n + 1) * 64],
                       start=False, stop=True, waits=(("dv", vpe),))
            last_mm = mm_op(em, B.psB[0:64, 0:64], kchunk(n), vchunk(n),
                            start=(n == 0), stop=(n == 63),
                            waits=(("dv", vpe),))
            vctx = tt_op(em, ctx_sb[:, n * 64:(n + 1) * 64], ps_o,
                         B.recip[:, n * 64:(n + 1) * 64], ALU.mult,
                         waits=(("pe", vo),) + ((buf_free["ctx"],) if n == 0 and "ctx" in buf_free else ()))
            ps_free[("o", par)] = ("dv", vctx)
        vst = em.op("gpsimd",
                    lambda e, o=ctx_d[hd * 64:(hd + 1) * 64, :], i=ctx_sb:
                    e.dma_start(out=o, in_=i), (("dv", vctx),), "st", 16)
        buf_free["ctx"] = ("st", vst)
    em.barrier()


def emit_proj_phase(em, B, d):
    x_in, x_out, ctx_d, wo_d = d["xin"], d["xout"], d["ctx"], d["wo"]
    stage_free = [None, None]
    vcasts = []
    for sl in range(8):
        s = sl % 2
        w = []
        if stage_free[s] is not None:
            w.append(stage_free[s])
        vld = em.dma("sync", B.wstage[:, s * 1024:(s + 1) * 1024],
                     wo_d[:, sl * 1024:(sl + 1) * 1024], waits=tuple(w))
        vc = cp_op(em, B.big[:, sl * 1024:(sl + 1) * 1024],
                   B.wstage[:, s * 1024:(s + 1) * 1024],
                   waits=(("ld", vld),), eng="gpsimd", sem="gp")
        vcasts.append(vc)
        stage_free[s] = ("gp", vc)
    vstore = None
    ps_free = {}
    for tt in range(NTT):
        x3 = B.xtt.rearrange("p (c t) -> p c t", c=8)
        xfree = [("st", vstore)] if vstore is not None else []
        vx = em.dma("sync", x3[:],
                    x_in.rearrange("(c p) t -> p c t", p=128)[:, :, tt * TT:(tt + 1) * TT],
                    waits=tuple(xfree))
        c3 = B.h.rearrange("p (c t) -> p c t", c=8)
        vc = em.dma("sync", c3[:],
                    ctx_d.rearrange("(c p) t -> p c t", p=128)[:, :, tt * TT:(tt + 1) * TT],
                    waits=tuple(xfree))
        vx_last = None
        for oc in range(8):
            ps = B.psA if oc % 2 == 0 else B.psB
            for hh in range(2):
                key = (oc % 2, hh)
                wm = [("ld", vc), ("gp", vcasts[oc])]
                if key in ps_free:
                    wm.append(ps_free[key])
                for dc in range(NFC):
                    vmm = mm_op(em, ps[:, hh * 512:(hh + 1) * 512],
                                B.big[:, oc * 1024 + dc * 128: oc * 1024 + dc * 128 + 128],
                                B.h[:, dc * TT + hh * 512: dc * TT + (hh + 1) * 512],
                                start=(dc == 0), stop=(dc == NFC - 1),
                                waits=tuple(wm) if dc == 0 else ())
                xs = B.xtt[:, oc * TT + hh * 512: oc * TT + (hh + 1) * 512]
                vx_last = tt_op(em, xs, xs, ps[:, hh * 512:(hh + 1) * 512], ALU.add,
                                waits=(("pe", vmm), ("ld", vx)))
                ps_free[key] = ("dv", vx_last)
        dst = x_out.rearrange("(c p) t -> p c t", p=128)[:, :, tt * TT:(tt + 1) * TT]
        vstore = em.op("gpsimd", lambda e, o=dst, i=x3[:]: e.dma_start(out=o, in_=i),
                       (("dv", vx_last),), "st", 16)
    em.barrier()


CONST_SPECS = [
    ("ones_f", [128, 128], F32), ("ones_b", [128, 128], BF16),
    ("ident", [128, 128], F32), ("sel9", [9, 9 * 128], BF16),
    ("sel8", [8, 8 * 128], BF16), ("triu", [64, 64], F32),
]

WEIGHT_SPECS = [
    ("m0wr", [128, 80], F32),
    ("m0wg", [128, 9 * 22 * 1024], F32),
    ("m0wu", [128, 9 * 22 * 1024], F32),
    ("m0wd", [128, 9 * 8 * 22 * 128], F32),
    ("wq0", [128, 24 * 1024], F32),
    ("m1wr", [128, 80], F32),
    ("m1wg", [128, 8 * 8 * 1024], F32),
    ("m1wu", [128, 8 * 8 * 1024], F32),
    ("m1wd", [128, 8 * 8 * 8 * 128], F32),
    ("wq1", [128, 24 * 1024], F32),
    ("wo1", [128, 8 * 1024], F32),
]


LAUNCH_PLAN = [
    ("moe0", {"xin": "X", "m0wr": "m0wr", "m0wg": "m0wg", "m0wu": "m0wu",
              "m0wd": "m0wd"}, {"xout": "X"}),
    ("attn0a", {"xin": "X", "wq": "wq0"}, {"qfm": "qfm", "kfm": "kfm",
                                           "vtm": "vtm"}),
    ("attn0b", {"xin": "X", "qfm": "qfm", "kfm": "kfm", "vtm": "vtm"},
     {"xout": "X"}),
    ("moe0", {"xin": "X", "m0wr": "m0wr", "m0wg": "m0wg", "m0wu": "m0wu",
              "m0wd": "m0wd"}, {"xout": "X"}),
    ("moe1", {"xin": "X", "m1wr": "m1wr", "m1wg": "m1wg", "m1wu": "m1wu",
              "m1wd": "m1wd"}, {"xout": "X"}),
    ("attn1a", {"xin": "X", "wq": "wq1"}, {"qfm": "qfm", "kfm": "kfm",
                                           "ktm": "ktm", "vtm": "vtm"}),
    ("attn1b", {"qfm": "qfm", "kfm": "kfm", "ktm": "ktm", "vtm": "vtm",
                "recip64": "recip64"}, {"ctx": "ctx"}),
    ("attn1c", {"xin": "X", "ctx": "ctx", "wo": "wo1"}, {"xout": "X"}),
    ("moe1", {"xin": "X", "m1wr": "m1wr", "m1wg": "m1wg", "m1wu": "m1wu",
              "m1wd": "m1wd"}, {"xout": "X"}),
]

PHASE_IO = {
    "moe0":   dict(ins=[("xin", [D, T], F32), ("m0wr", [128, 80], F32),
                        ("m0wg", [128, 9 * 22 * 1024], F32),
                        ("m0wu", [128, 9 * 22 * 1024], F32),
                        ("m0wd", [128, 9 * 8 * 22 * 128], F32)],
                   outs=[("xout", [D, T], F32)],
                   internal=[("m0wgb", [128, 9 * 22 * 1024], BF16),
                             ("m0wub", [128, 9 * 22 * 1024], BF16),
                             ("m0wdb", [128, 9 * 8 * 22 * 128], BF16)]),
    "attn0a": dict(ins=[("xin", [D, T], F32), ("wq", [128, 24 * 1024], F32)],
                   outs=[("qfm", [D, T], BF16), ("kfm", [D, T], BF16),
                         ("vtm", [T, D], BF16)]),
    "attn0b": dict(ins=[("xin", [D, T], F32), ("qfm", [D, T], BF16),
                        ("kfm", [D, T], BF16), ("vtm", [T, D], BF16)],
                   outs=[("xout", [D, T], F32)]),
    "moe1":   dict(ins=[("xin", [D, T], F32), ("m1wr", [128, 80], F32),
                        ("m1wg", [128, 8 * 8 * 1024], F32),
                        ("m1wu", [128, 8 * 8 * 1024], F32),
                        ("m1wd", [128, 8 * 8 * 8 * 128], F32)],
                   outs=[("xout", [D, T], F32)],
                   internal=[("m1wgb", [128, 8 * 8 * 1024], BF16),
                             ("m1wub", [128, 8 * 8 * 1024], BF16),
                             ("m1wdb", [128, 8 * 8 * 8 * 128], BF16)]),
    "attn1a": dict(ins=[("xin", [D, T], F32), ("wq", [128, 24 * 1024], F32)],
                   outs=[("qfm", [D, T], BF16), ("kfm", [D, T], BF16),
                         ("ktm", [T, D], BF16), ("vtm", [T, D], BF16)]),
    "attn1b": dict(ins=[("qfm", [D, T], BF16), ("kfm", [D, T], BF16),
                        ("ktm", [T, D], BF16), ("vtm", [T, D], BF16),
                        ("recip64", [64, T], F32)],
                   outs=[("ctx", [D, T], BF16)]),
    "attn1c": dict(ins=[("xin", [D, T], F32), ("ctx", [D, T], BF16),
                        ("wo", [128, 8 * 1024], F32)],
                   outs=[("xout", [D, T], F32)]),
}


def emit_phase(em, B, name, d):
    if name == "moe0":
        emit_moe_phase(em, B, d, 9, 22, "m0")
    elif name == "moe1":
        emit_moe_phase(em, B, d, 8, 8, "m1")
    elif name == "attn0a":
        emit_qkv_phase(em, B, d, 1.0 / 32.0, False)
    elif name == "attn0b":
        emit_attn0_phase(em, B, d)
    elif name == "attn1a":
        emit_qkv_phase(em, B, d, 1.0 / 8.0, True)
    elif name == "attn1b":
        emit_attn1_phase(em, B, d)
    elif name == "attn1c":
        emit_proj_phase(em, B, d)
    else:
        raise ValueError(name)


def build_phase_program(name, serial=False):
    nc = bass.Bass("TRN2", target_bir_lowering=False)
    io = PHASE_IO[name]
    d = {}
    for nm, shp, dt in CONST_SPECS:
        d[nm] = nc.dram_tensor(nm, shp, dt, kind="ExternalInput").ap()
    for nm, shp, dt in io["ins"]:
        d[nm] = nc.dram_tensor(nm, shp, dt, kind="ExternalInput").ap()
    for nm, shp, dt in io["outs"]:
        d[nm] = nc.dram_tensor(nm, shp, dt, kind="ExternalOutput").ap()
    for nm, shp, dt in io.get("internal", []):
        d[nm] = nc.dram_tensor(nm, shp, dt, kind="Internal").ap()
    with contextlib.ExitStack() as stack:
        B = alloc_bufs(nc, stack)
        em = Em(nc, serial=serial)
        em.sems(stack, "ld", "st", "pe", "dv", "ac", "gp")
        block = stack.enter_context(nc.Block())
        load_consts(em, B, d, recip=(name == "attn1b"))
        emit_phase(em, B, name, d)
        em.op("gpsimd", None, (("st", em.cnt["st"]),))
        em.emit(block)
    return nc


def build_full_program(serial=False):
    nc = bass.Bass("TRN2", target_bir_lowering=False)
    d = {}
    for nm, shp, dt in CONST_SPECS:
        d[nm] = nc.dram_tensor(nm, shp, dt, kind="ExternalInput").ap()
    d["recip64"] = nc.dram_tensor("recip64", [64, T], F32,
                                  kind="ExternalInput").ap()
    for nm, shp, dt in WEIGHT_SPECS:
        d[nm] = nc.dram_tensor(nm, shp, dt, kind="ExternalInput").ap()
    XIN = nc.dram_tensor("xin", [D, T], F32, kind="ExternalInput").ap()
    XOUT = nc.dram_tensor("xout", [D, T], F32, kind="ExternalOutput").ap()
    XA = nc.dram_tensor("xa", [D, T], F32, kind="Internal").ap()
    XB = nc.dram_tensor("xb", [D, T], F32, kind="Internal").ap()
    qfm = nc.dram_tensor("qfm", [D, T], BF16, kind="Internal").ap()
    kfm = nc.dram_tensor("kfm", [D, T], BF16, kind="Internal").ap()
    ktm = nc.dram_tensor("ktm", [T, D], BF16, kind="Internal").ap()
    vtm = nc.dram_tensor("vtm", [T, D], BF16, kind="Internal").ap()
    ctx = nc.dram_tensor("ctx", [D, T], BF16, kind="Internal").ap()
    for nm, shp in (("m0wgb", [128, 9 * 22 * 1024]), ("m0wub", [128, 9 * 22 * 1024]),
                    ("m0wdb", [128, 9 * 8 * 22 * 128]),
                    ("m1wgb", [128, 8 * 8 * 1024]), ("m1wub", [128, 8 * 8 * 1024]),
                    ("m1wdb", [128, 8 * 8 * 8 * 128])):
        d[nm] = nc.dram_tensor(nm, shp, BF16, kind="Internal").ap()
    m0 = {k: d[k] for k in ("m0wr", "m0wg", "m0wu", "m0wd",
                            "m0wgb", "m0wub", "m0wdb")}
    m1 = {k: d[k] for k in ("m1wr", "m1wg", "m1wu", "m1wd",
                            "m1wgb", "m1wub", "m1wdb")}
    att = dict(qfm=qfm, kfm=kfm, vtm=vtm)
    att1 = dict(qfm=qfm, kfm=kfm, ktm=ktm, vtm=vtm)
    with contextlib.ExitStack() as stack:
        B = alloc_bufs(nc, stack)
        em = Em(nc, serial=serial)
        em.sems(stack, "ld", "st", "pe", "dv", "ac", "gp")
        block = stack.enter_context(nc.Block())
        load_consts(em, B, d, recip=False)

        emit_moe_phase(em, B, dict(xin=XIN, xout=XA, **m0), 9, 22, "m0")
        em.new_phase(stack)
        emit_qkv_phase(em, B, dict(xin=XA, wq=d["wq0"], **att), 1.0 / 32.0, False)
        em.new_phase(stack)
        emit_attn0_phase(em, B, dict(xin=XA, xout=XB, **att))
        em.new_phase(stack)
        emit_moe_phase(em, B, dict(xin=XB, xout=XA, **m0), 9, 22, "m0", first=False)
        em.new_phase(stack)
        emit_moe_phase(em, B, dict(xin=XA, xout=XB, **m1), 8, 8, "m1")
        em.new_phase(stack)
        emit_qkv_phase(em, B, dict(xin=XB, wq=d["wq1"], **att1), 1.0 / 8.0, True)
        em.new_phase(stack)
        em.dma("sync", B.recip[:], d["recip64"][:])
        em.barrier()
        emit_attn1_phase(em, B, dict(ctx=ctx, **att1))
        em.new_phase(stack)
        emit_proj_phase(em, B, dict(xin=XB, xout=XA, ctx=ctx, wo=d["wo1"]))
        em.new_phase(stack)
        emit_moe_phase(em, B, dict(xin=XA, xout=XOUT, **m1), 8, 8, "m1", first=False)
        em.op("gpsimd", None, (("st", em.cnt["st"]),))
        em.emit(block)
    return nc


def swz_nmajor(W):
    K, N = W.shape
    kc, nb = K // 128, N // 128
    A = W.reshape(kc, 128, nb, 128).transpose(1, 2, 0, 3)
    return np.ascontiguousarray(A.reshape(128, nb * kc * 128))


def swz_router(Wr, nE):
    A = Wr.reshape(8, 128, nE).transpose(1, 0, 2)
    out = np.zeros((128, 80), dtype=np.float32)
    out.reshape(128, 8, 10)[:, :, :nE] = A
    return out


def make_consts():
    c = {}
    c["ones_f"] = np.ones((128, 128), np.float32)
    c["ones_b"] = np.ones((128, 128), BFNP)
    c["ident"] = np.eye(128, dtype=np.float32)
    s9 = np.zeros((9, 9 * 128), BFNP)
    for e in range(9):
        s9[e, e * 128:(e + 1) * 128] = 1
    c["sel9"] = s9
    s8 = np.zeros((8, 8 * 128), BFNP)
    for e in range(8):
        s8[e, e * 128:(e + 1) * 128] = 1
    c["sel8"] = s8
    tri = np.zeros((64, 64), np.float32)
    for j in range(64):
        tri[j, j:] = 1.0
    c["triu"] = tri
    rc = (1.0 / np.arange(1, T + 1, dtype=np.float64)).astype(np.float32)
    c["recip64"] = np.ascontiguousarray(np.broadcast_to(rc[None, :], (64, T)))
    return c


def prep_weights(inp):
    w = {}
    w["m0wr"] = swz_router(np.asarray(inp["L0_ffn_Wr"], np.float32), 9)
    Wg = np.asarray(inp["L0_ffn_Wg"], np.float32)
    Wu = np.asarray(inp["L0_ffn_Wu"], np.float32)
    Wd = np.asarray(inp["L0_ffn_Wd"], np.float32)
    w["m0wg"] = np.concatenate([swz_nmajor(Wg[e]) for e in range(9)], axis=1)
    w["m0wu"] = np.concatenate([swz_nmajor(Wu[e]) for e in range(9)], axis=1)
    w["m0wd"] = np.concatenate([swz_nmajor(Wd[e]) for e in range(9)], axis=1)
    w["wq0"] = swz_nmajor(np.asarray(inp["L0_attn_w_qkv"], np.float32))
    w["m1wr"] = swz_router(np.ascontiguousarray(
        np.asarray(inp["L1_ffn_router_w"], np.float32).T), 8)
    up = np.asarray(inp["L1_ffn_up_proj"], np.float32)
    dn = np.asarray(inp["L1_ffn_down_proj"], np.float32)
    w["m1wg"] = np.concatenate([swz_nmajor(up[e][:, :D]) for e in range(8)], axis=1)
    w["m1wu"] = np.concatenate([swz_nmajor(up[e][:, D:]) for e in range(8)], axis=1)
    w["m1wd"] = np.concatenate([swz_nmajor(dn[e]) for e in range(8)], axis=1)
    w["wq1"] = swz_nmajor(np.ascontiguousarray(
        np.asarray(inp["L1_attn_W_qkv"], np.float32).T))
    w["wo1"] = swz_nmajor(np.ascontiguousarray(
        np.asarray(inp["L1_attn_W_o"], np.float32).T))
    return w


def kernel(**inputs):
    consts = make_consts()
    w = prep_weights(inputs)
    x = np.asarray(inputs["L0_attn_embeddings"], np.float32)
    if "full" not in _PROG_CACHE:
        _PROG_CACHE["full"] = build_full_program()
    nc = _PROG_CACHE["full"]
    base = dict(consts)
    base.update(w)
    in_maps = []
    for i in range(NCORES):
        m = dict(base)
        m["xin"] = np.ascontiguousarray(x[i].T)
        in_maps.append(m)
    res = run_bass_kernel_spmd(nc, in_maps, core_ids=list(range(NCORES)))
    out = np.stack([np.asarray(res.results[i]["xout"]).T
                    for i in range(NCORES)], axis=0)
    return np.ascontiguousarray(out.astype(np.float32))
